# version 10
# baseline (speedup 1.0000x reference)
import numpy as np
import concourse.bass as bass
import concourse.mybir as mybir
from concourse.bass_utils import run_bass_kernel_spmd
from contextlib import ExitStack

F32 = mybir.dt.float32
AF = mybir.ActivationFunctionType
ALU = mybir.AluOpType
AXX = mybir.AxisListType.X

T = 2048
C = 1024
CH = 8
NE = 8
NHP = 8
TB = 4
KC = 16
HD = 512
HC4 = HD // 128
ATT_SCALE = float(1.0 / np.sqrt(32.0))
FFN_SCALE = 0.125


class Ctr:
    def __init__(self):
        self.v = {}

    def inc(self, k, n=1):
        self.v[k] = self.v.get(k, 0) + n
        return self.v[k]

    def get(self, k):
        return self.v.get(k, 0)


def build_nc(do_header=4, do_attn=True, n_experts=NE, dbg_gates=False):
    do_header = int(do_header)
    nc = bass.Bass("TRN2", target_bir_lowering=False)
    din = lambda n, s: nc.dram_tensor(n, s, F32, kind="ExternalInput").ap()

    xT = din("xT", [C, T])
    wqT, wkT, wvT, wpT = (din(n, [C, C]) for n in ("wqT", "wkT", "wvT", "wpT"))
    bp = din("bp", [128, CH])
    bn_mean, bn_var, bn_g, bn_b = (din(n, [128, CH]) for n in ("bn_mean", "bn_var", "bn_g", "bn_b"))
    rw1T = din("rw1T", [C, HD])
    rb1 = din("rb1", [128, HC4])
    lng, lnb = din("lng", [128, HC4]), din("lnb", [128, HC4])
    rw2T = din("rw2T", [128, HC4, NE])
    rb2 = din("rb2", [1, NE])
    qWT, kWT, vWT, oWT = (din(n, [NE, C, C]) for n in ("qWT", "kWT", "vWT", "oWT"))
    qbD, kbD, vbD, obD = (din(n, [NE, 128, CH]) for n in ("qb", "kb", "vb", "ob"))
    vbR = din("vbR", [NE, C])
    identD = din("ident", [128, 128])
    yT = nc.dram_tensor("yT", [C, T], F32, kind="ExternalOutput").ap()
    gdbg = nc.dram_tensor("gdbg", [NE, T], F32, kind="ExternalOutput").ap() if dbg_gates else None
    ldbg = nc.dram_tensor("ldbg", [128, 16, NE], F32, kind="ExternalOutput").ap() if dbg_gates else None
    hdbg = nc.dram_tensor("hdbg", [128, T], F32, kind="ExternalOutput").ap() if dbg_gates else None
    sdbg = nc.dram_tensor("sdbg", [128, T], F32, kind="ExternalOutput").ap() if dbg_gates else None
    bdbg = nc.dram_tensor("bdbg", [128, T], F32, kind="ExternalOutput").ap() if dbg_gates else None
    xndbg = nc.dram_tensor("xndbg", [128, T], F32, kind="ExternalOutput").ap() if dbg_gates else None
    hrdbg = nc.dram_tensor("hrdbg", [128, T], F32, kind="ExternalOutput").ap() if dbg_gates else None
    g16dbg = nc.dram_tensor("g16dbg", [128, 16, NE], F32, kind="ExternalOutput").ap() if dbg_gates else None
    hscr = nc.dram_tensor("hscr", [HD, T], F32, kind="Internal").ap()

    cc = Ctr()

    def vslice_ap(t, e_off=0, n=C):
        return bass.AP(t.tensor, e_off, [[n // 128, 128], [1, n // 128]])

    def colslice_ap(w, base_elem, nin=C):
        return bass.AP(w.tensor, base_elem, [[w.tensor.shape[-1], 128],
                                             [w.tensor.shape[-1] * 128, nin // 128],
                                             [1, 128]])

    with ExitStack() as ctx:
        sb = lambda n, s, d=F32: ctx.enter_context(nc.sbuf_tensor(n, s, d))
        ps = lambda n, s: ctx.enter_context(nc.psum_tensor(n, s, F32))
        sem = lambda n: ctx.enter_context(nc.semaphore(n))

        X = sb("X", [128, CH, T])
        XN = sb("XN", [128, CH, T])
        gT8 = sb("gT8", [NE, T])
        g16 = sb("g16", [128, 16, NE])
        grow = sb("grow", [1, T])
        rrow = sb("rrow", [1, 2, 512])
        onesc = sb("onesc", [128, 1])
        onesr = sb("onesr", [1, 128])
        vbRow = sb("vbRow", [1, 128])
        zcol = sb("zcol", [128, 1])
        bpS = sb("bpS", [128, CH])
        bnA = sb("bnA", [128, CH])
        bnB = sb("bnB", [128, CH])
        bnM = sb("bnM", [128, CH])
        bnG = sb("bnG", [128, CH])
        rb1S = sb("rb1S", [128, HC4])
        lngS = sb("lngS", [128, HC4])
        lnbS = sb("lnbS", [128, HC4])
        r2S = sb("r2S", [128, HC4, NE])
        rb2r = sb("rb2r", [128, NE])
        qbS = sb("qbS", [128, CH])
        kbS = sb("kbS", [128, CH])
        vbS = sb("vbS", [128, CH])
        obS = sb("obS", [128, CH])
        qbs2 = sb("qbs2", [128, CH])
        ob2 = sb("ob2", [128, CH])
        lgS = sb("lgS", [128, 16, NE])
        gmA = sb("gmA", [128, 16])
        gmB = sb("gmB", [128, 16])
        gw1 = sb("gw1", [128, NE])

        pA0 = ps("pA0", [128, 512])
        pA1 = ps("pA1", [128, 512])
        pB0 = ps("pB0", [128, 512])
        pB1 = ps("pB1", [128, 512])
        pC0 = ps("pC0", [65, 512])
        pC1 = ps("pC1", [65, 512])
        pD = ps("pD", [128, 512])
        pE = ps("pE", [128, 512])

        sLd = sem("sLd")
        sBc = sem("sBc")
        sMm = sem("sMm")
        sEx = sem("sEx")
        sEv = sem("sEv")
        sPv = sem("sPv")
        sSt = sem("sSt")
        sWt = sem("sWt")
        sWa = sem("sWa")
        sWb = sem("sWb")
        sWo = sem("sWo")
        sStg = sem("sStg")

        with nc.Block() as blk:
            @blk.sync
            def _(sync):
                sync.dma_start(out=X[:], in_=bass.AP(xT.tensor, 0, [[T, 128], [T * 128, CH], [1, T]]))\
                    .then_inc(sLd, 16); cc.inc("ld", 16)
                for dst, src in ((bpS, bp), (bnA, bn_var), (bnB, bn_b), (bnM, bn_mean), (bnG, bn_g)):
                    sync.dma_start(out=dst[:], in_=vslice_ap(src)).then_inc(sLd, 16); cc.inc("ld", 16)
                for dst, src in ((rb1S, rb1), (lngS, lng), (lnbS, lnb)):
                    sync.dma_start(out=dst[:], in_=vslice_ap(src, 0, HD)).then_inc(sLd, 16); cc.inc("ld", 16)
                sync.dma_start(out=r2S[:], in_=rw2T[:, :, :]).then_inc(sLd, 16); cc.inc("ld", 16)
                sync.dma_start(out=rb2r[0:1, :], in_=rb2[:, :]).then_inc(sLd, 16); cc.inc("ld", 16)

            @blk.vector
            def _(v):
                v.memset(onesc[:], 1.0)
                v.memset(onesr[:], 1.0)
                v.memset(zcol[:], 0.0)

            @blk.gpsimd
            def _(g):
                g.wait_ge(sLd, cc.get("ld"))
                g.dma_start(out=rb2r[1:128, :], in_=bass.AP(rb2r, 0, [[NE, 1], [0, 127], [1, NE]]))\
                    .then_inc(sBc, 16); cc.inc("bc", 16)

        with ExitStack() as ph:
            S_rep = ph.enter_context(nc.sbuf_tensor("S_rep", [128, T], F32))
            B_rep = ph.enter_context(nc.sbuf_tensor("B_rep", [128, T], F32))
            rowsW = ph.enter_context(nc.sbuf_tensor("rowsW", [128, 4, 512], F32))
            scr4 = ph.enter_context(nc.sbuf_tensor("scr4", [128, 2, 512], F32))
            Wh = ph.enter_context(nc.sbuf_tensor("Wh", [128, CH, 128], F32))
            happ = ph.enter_context(nc.sbuf_tensor("happ", [128, T], F32))
            idS = ph.enter_context(nc.sbuf_tensor("idS", [128, 128], F32))

            with nc.Block() as blk:
                @blk.sync
                def _(sync):
                    sync.dma_start(out=idS[:], in_=identD[:, :]).then_inc(sLd, 16); cc.inc("ld", 16)

            with nc.Block() as blk:
                @blk.tensor
                def _(t):
                    t.wait_ge(sLd, cc.get("ld"))
                    for j in range(4):
                        for c in range(CH):
                            t.matmul(pB0[32 * j:32 * j + 1, :], onesc[:],
                                     X[:, c, j * 512:(j + 1) * 512],
                                     start=(c == 0), stop=(c == CH - 1),
                                     tile_position=(0, 32 * j), skip_group_check=True)

            with nc.Block() as blk:
                mm0, ex0 = cc.get("mm"), cc.get("ex")

                @blk.scalar
                def _(s):
                    k = 0
                    for c in range(CH):
                        for q4 in range(4):
                            if k >= 2:
                                s.wait_ge(sMm, mm0 + k - 1)
                            s.activation(scr4[:, k % 2, :], X[:, c, q4 * 512:(q4 + 1) * 512], AF.Square)
                            s.drain().then_inc(sEx, 1)
                            k += 1

                @blk.tensor
                def _(t):
                    k = 0
                    for c in range(CH):
                        for q4 in range(4):
                            t.wait_ge(sEx, ex0 + k + 1)
                            t.matmul(pB1[32 * q4:32 * q4 + 1, :], onesc[:], scr4[:, k % 2, :],
                                     start=(c == 0), stop=(c == CH - 1),
                                     tile_position=(0, 32 * q4), skip_group_check=True).then_inc(sMm, 1)
                            k += 1
                cc.inc("ex", 32)
                cc.inc("mm", 32)

            def ln_rows_and_bcast(px, pxx, inv, eps):
                with nc.Block() as blk:
                    @blk.vector
                    def _(v):
                        P4 = [slice(32 * j, 32 * j + 1) for j in range(4)]
                        for p in P4:
                            v.tensor_single_scalar(rowsW[p, 0, :], px[p, :], inv, ALU.mult)
                            v.tensor_single_scalar(rowsW[p, 1, :], pxx[p, :], inv, ALU.mult)
                        v.drain()
                        for p in P4:
                            v.tensor_tensor(rowsW[p, 3, :], rowsW[p, 0, :], rowsW[p, 0, :], ALU.mult)
                        v.drain()
                        for p in P4:
                            v.tensor_tensor(rowsW[p, 1, :], rowsW[p, 1, :], rowsW[p, 3, :], ALU.subtract)
                        v.drain()
                        for p in P4:
                            v.tensor_single_scalar(rowsW[p, 1, :], rowsW[p, 1, :], eps, ALU.add)
                with nc.Block() as blk:
                    @blk.scalar
                    def _(s):
                        for j in range(4):
                            p = slice(32 * j, 32 * j + 1)
                            s.activation(rowsW[p, 2, :], rowsW[p, 1, :], AF.Sqrt)
                with nc.Block() as blk:
                    @blk.vector
                    def _(v):
                        P4 = [slice(32 * j, 32 * j + 1) for j in range(4)]
                        for p in P4:
                            v.reciprocal(rowsW[p, 2, :], rowsW[p, 2, :])
                        v.drain()
                        for p in P4:
                            v.tensor_tensor(rowsW[p, 3, :], rowsW[p, 2, :], rowsW[p, 2, :], ALU.mult)
                        v.drain()
                        for p in P4:
                            v.tensor_tensor(rowsW[p, 3, :], rowsW[p, 3, :], rowsW[p, 1, :], ALU.mult)
                        v.drain()
                        for p in P4:
                            v.tensor_scalar(rowsW[p, 3, :], rowsW[p, 3, :], -0.5, 1.5, ALU.mult, ALU.add)
                        v.drain()
                        for p in P4:
                            v.tensor_tensor(rowsW[p, 2, :], rowsW[p, 2, :], rowsW[p, 3, :], ALU.mult)
                        v.drain()
                        for p in P4:
                            v.scalar_tensor_tensor(rowsW[p, 3, :], rowsW[p, 0, :], -1.0,
                                                   rowsW[p, 2, :], ALU.mult, ALU.mult)
                with nc.Block() as blk:
                    @blk.gpsimd
                    def _(g):
                        g.wait_ge(sStg, cc.get("stg"))
                        for j in range(4):
                            sl = slice(j * 512, (j + 1) * 512)
                            g.dma_start(out=S_rep[:, sl],
                                        in_=bass.AP(rowsW, 32 * j * 2048 + 2 * 512, [[2048, 1], [0, 128], [1, 512]]))\
                                .then_inc(sBc, 16); cc.inc("bc", 16)
                            g.dma_start(out=B_rep[:, sl],
                                        in_=bass.AP(rowsW, 32 * j * 2048 + 3 * 512, [[2048, 1], [0, 128], [1, 512]]))\
                                .then_inc(sBc, 16); cc.inc("bc", 16)
                with nc.Block() as blk:
                    @blk.vector
                    def _(v):
                        v.wait_ge(sBc, cc.get("bc"))

            ln_rows_and_bcast(pB0, pB1, 1.0 / 1024.0, 1e-5)

            with nc.Block() as blk:
                @blk.vector
                def _(v):
                    for c in range(CH):
                        v.tensor_tensor(XN[:, c, :], X[:, c, :], S_rep[:], ALU.mult)
                    v.drain()
                    for c in range(CH):
                        v.tensor_tensor(XN[:, c, :], XN[:, c, :], B_rep[:], ALU.add)
                    v.drain()
                    for c in range(CH):
                        v.tensor_single_scalar(X[:, c, :], X[:, c, :], bpS[:, c:c + 1], ALU.add)

            stage_ld = {}
            for hc in range(HC4 if do_header >= 1 else 0):
                with nc.Block() as blk:
                    @blk.sync
                    def _(sync, hc=hc):
                        sync.dma_start(out=Wh[:], in_=colslice_ap(rw1T, hc * 128))\
                            .then_inc(sLd, 16); cc.inc("ld", 16)
                with nc.Block() as blk:
                    mm0, ex0 = cc.get("mm"), cc.get("ex")

                    @blk.tensor
                    def _(t, hc=hc, mm0=mm0, ex0=ex0):
                        t.wait_ge(sLd, cc.get("ld"))
                        if hc > 0:
                            t.wait_ge(sStg, stage_ld[hc - 1])
                        for tb in range(TB):
                            if tb >= 2:
                                t.wait_ge(sEx, ex0 + tb - 1)
                            bank = (pA0, pA1)[tb % 2]
                            for ic in range(CH):
                                t.matmul(bank[:, :], Wh[:, ic, :], XN[:, ic, tb * 512:(tb + 1) * 512],
                                         start=(ic == 0), stop=(ic == CH - 1))
                            t.drain().then_inc(sMm, 1)

                    @blk.scalar
                    def _(s, hc=hc, mm0=mm0):
                        for tb in range(TB):
                            s.wait_ge(sMm, mm0 + tb + 1)
                            bank = (pA0, pA1)[tb % 2]
                            s.activation(B_rep[:, tb * 512:(tb + 1) * 512], bank[:, :], AF.Identity,
                                         bias=rb1S[:, hc:hc + 1], scale=1.0).then_inc(sEx, 1)
                    cc.inc("mm", TB)
                    cc.inc("ex", TB)

                with nc.Block() as blk:
                    @blk.sync
                    def _(sync, hc=hc):
                        sync.dma_start(out=bass.AP(hscr.tensor, hc * 128 * T, [[T, 128], [1, T]]),
                                       in_=B_rep[:, :]).then_inc(sStg, 16)
                        cc.inc("stg", 16)
                        if dbg_gates and hc == 0:
                            sync.dma_start(out=hdbg[:, :], in_=B_rep[:, :]).then_inc(sLd, 16)
                            cc.inc("ld", 16)
                    stage_ld[hc] = cc.get("stg")

                    if do_header >= 2:
                        @blk.tensor
                        def _(t, hc=hc):
                            for j in range(4):
                                t.matmul(pB0[32 * j:32 * j + 1, :], onesc[:],
                                         B_rep[:, j * 512:(j + 1) * 512],
                                         start=(hc == 0), stop=(hc == HC4 - 1),
                                         tile_position=(0, 32 * j), skip_group_check=True)

                if do_header >= 2:
                    with nc.Block() as blk:
                        mm0, ex0 = cc.get("mm"), cc.get("ex")

                        @blk.scalar
                        def _(s, hc=hc, mm0=mm0):
                            for q4 in range(4):
                                if q4 >= 2:
                                    s.wait_ge(sMm, mm0 + q4 - 1)
                                s.activation(scr4[:, q4 % 2, :], B_rep[:, q4 * 512:(q4 + 1) * 512], AF.Square)
                                s.drain().then_inc(sEx, 1)

                        @blk.tensor
                        def _(t, hc=hc, ex0=ex0):
                            for q4 in range(4):
                                t.wait_ge(sEx, ex0 + q4 + 1)
                                t.matmul(pB1[32 * q4:32 * q4 + 1, :], onesc[:], scr4[:, q4 % 2, :],
                                         start=(hc == 0), stop=(hc == HC4 - 1),
                                         tile_position=(0, 32 * q4), skip_group_check=True)
                                t.drain().then_inc(sMm, 1)
                        cc.inc("mm", 4)
                        cc.inc("ex", 4)

            if do_header >= 3:
                ln_rows_and_bcast(pB0, pB1, 1.0 / HD, 1e-5)

            if dbg_gates and do_header >= 3:
                with nc.Block() as blk:
                    @blk.sync
                    def _(sync):
                        sync.dma_start(out=sdbg[:, :], in_=S_rep[:, :]).then_inc(sLd, 16)
                        cc.inc("ld", 16)
                        sync.dma_start(out=bdbg[:, :], in_=B_rep[:, :]).then_inc(sLd, 16)
                        cc.inc("ld", 16)
                        sync.dma_start(out=xndbg[:, :], in_=XN[:, 0, :]).then_inc(sLd, 16)
                        cc.inc("ld", 16)
            for hc in range(HC4 if do_header >= 3 else 0):
                with nc.Block() as blk:
                    @blk.sync
                    def _(sync, hc=hc):
                        sync.wait_ge(sStg, cc.get("stg"))
                        sync.dma_start(out=happ[:, :],
                                       in_=bass.AP(hscr.tensor, hc * 128 * T, [[T, 128], [1, T]]))\
                            .then_inc(sLd, 16); cc.inc("ld", 16)
                with nc.Block() as blk:
                    @blk.vector
                    def _(v, hc=hc):
                        v.wait_ge(sLd, cc.get("ld"))
                        v.tensor_tensor(happ[:, :], happ[:, :], S_rep[:, :], ALU.mult)
                        v.drain()
                        v.tensor_tensor(happ[:, :], happ[:, :], B_rep[:, :], ALU.add)
                with nc.Block() as blk:
                    @blk.scalar
                    def _(s, hc=hc):
                        s.activation(happ[:, :], happ[:, :], AF.Relu,
                                     bias=lnbS[:, hc:hc + 1], scale=lngS[:, hc:hc + 1])
                if dbg_gates and hc == 0:
                    with nc.Block() as blk:
                        @blk.sync
                        def _(sync, hc=hc):
                            sync.dma_start(out=hrdbg[:, :], in_=happ[:, :]).then_inc(sLd, 16)
                            cc.inc("ld", 16)
                with nc.Block() as blk:
                    mm0, ev0 = cc.get("mm"), cc.get("ev")

                    @blk.tensor
                    def _(t, hc=hc, ev0=ev0):
                        for tt in range(16):
                            if tt >= 2:
                                t.wait_ge(sEv, ev0 + tt - 1)
                            bank = (pD, pE)[tt % 2]
                            t.matmul(bank[:, 0:8], happ[:, tt * 128:(tt + 1) * 128],
                                     r2S[:, hc, :], start=True, stop=True,
                                     skip_group_check=True).then_inc(sMm, 1)

                    @blk.vector
                    def _(v, hc=hc, mm0=mm0):
                        for tt in range(16):
                            v.wait_ge(sMm, mm0 + tt + 1)
                            bank = (pD, pE)[tt % 2]
                            if hc == 0:
                                v.tensor_copy(lgS[:, tt, :], bank[:, 0:8])
                                v.drain().then_inc(sEv, 1)
                            else:
                                v.tensor_tensor(lgS[:, tt, :], lgS[:, tt, :], bank[:, 0:8], ALU.add)
                                v.drain().then_inc(sEv, 1)
                    cc.inc("mm", 16)
                    cc.inc("ev", 16)

            if do_header >= 4:
                with nc.Block() as blk:
                    @blk.vector
                    def _(v):
                        for tt in range(16):
                            v.tensor_tensor(lgS[:, tt, :], lgS[:, tt, :], rb2r[:, :], ALU.add)
                with nc.Block() as blk:
                    @blk.vector
                    def _(v):
                        for tt in range(16):
                            v.reduce_max(gmA[:, tt:tt + 1], lgS[:, tt, :], axis=AXX)
                with nc.Block() as blk:
                    @blk.vector
                    def _(v):
                        for tt in range(16):
                            v.tensor_tensor(g16[:, tt, :], lgS[:, tt, :],
                                            bass.AP(gmA, tt, [[16, 128], [0, NE]]), ALU.is_equal)
                        v.drain()
                        for tt in range(16):
                            v.tensor_single_scalar(g16[:, tt, :], g16[:, tt, :], -1e30, ALU.mult)
                        v.drain()
                        for tt in range(16):
                            v.tensor_tensor(g16[:, tt, :], lgS[:, tt, :], g16[:, tt, :], ALU.add)
                with nc.Block() as blk:
                    @blk.vector
                    def _(v):
                        for tt in range(16):
                            v.reduce_max(gmB[:, tt:tt + 1], g16[:, tt, :], axis=AXX)
                with nc.Block() as blk:
                    @blk.vector
                    def _(v):
                        for tt in range(16):
                            v.tensor_single_scalar(gmA[:, tt:tt + 1], gmA[:, tt:tt + 1], -1.0, ALU.mult)
                with nc.Block() as blk:
                    @blk.scalar
                    def _(s):
                        for tt in range(16):
                            s.activation(g16[:, tt, :], lgS[:, tt, :], AF.Exp,
                                         bias=gmA[:, tt:tt + 1], scale=1.0)
                with nc.Block() as blk:
                    @blk.vector
                    def _(v):
                        for tt in range(16):
                            v.tensor_tensor(lgS[:, tt, :], lgS[:, tt, :],
                                            bass.AP(gmB, tt, [[16, 128], [0, NE]]), ALU.is_ge)
                        v.drain()
                        for tt in range(16):
                            v.tensor_tensor(g16[:, tt, :], g16[:, tt, :], lgS[:, tt, :], ALU.mult)
                with nc.Block() as blk:
                    @blk.vector
                    def _(v):
                        for tt in range(16):
                            v.reduce_sum(gmB[:, tt:tt + 1], g16[:, tt, :], axis=AXX)
                with nc.Block() as blk:
                    @blk.vector
                    def _(v):
                        for tt in range(16):
                            v.reciprocal(gmB[:, tt:tt + 1], gmB[:, tt:tt + 1])
                with nc.Block() as blk:
                    @blk.vector
                    def _(v):
                        for tt in range(16):
                            v.tensor_tensor(g16[:, tt, :], g16[:, tt, :],
                                            bass.AP(gmB, tt, [[16, 128], [0, NE]]), ALU.mult)

                for tt in range(16):
                    with nc.Block() as blk:
                        @blk.tensor
                        def _(t, tt=tt):
                            t.transpose(pE[0:NE, 0:128], g16[:, tt, :], idS[:])
                    with nc.Block() as blk:
                        @blk.vector
                        def _(v, tt=tt):
                            v.tensor_copy(gT8[:, tt * 128:(tt + 1) * 128], pE[0:NE, 0:128])

        if do_attn:
            with ExitStack() as ph:
                XR = ph.enter_context(nc.sbuf_tensor("XR", [128, CH, 512], F32))
                Wa = ph.enter_context(nc.sbuf_tensor("Wa", [128, CH, 128], F32))
                Wb = ph.enter_context(nc.sbuf_tensor("Wb", [128, CH, 128], F32))
                WpS = ph.enter_context(nc.sbuf_tensor("WpS", [128, C], F32))
                ktc = ph.enter_context(nc.sbuf_tensor("ktc", [128, 4, 128], F32))
                vtc = ph.enter_context(nc.sbuf_tensor("vtc", [128, 4, 128], F32))
                qaA = ph.enter_context(nc.sbuf_tensor("qaA", [128, T], F32))
                Gbd = ph.enter_context(nc.sbuf_tensor("Gbd", [128, 128], F32))
                ocA = ph.enter_context(nc.sbuf_tensor("ocA", [128, T], F32))

                with nc.Block() as blk:
                    @blk.vector
                    def _(v):
                        v.tensor_single_scalar(bnA[:], bnA[:], 1e-6, ALU.add)
                with nc.Block() as blk:
                    @blk.scalar
                    def _(s):
                        s.activation(qbs2[:], bnA[:], AF.Sqrt)
                with nc.Block() as blk:
                    @blk.vector
                    def _(v):
                        v.reciprocal(qbs2[:], qbs2[:])
                        v.drain()
                        v.tensor_tensor(ob2[:], qbs2[:], qbs2[:], ALU.mult)
                        v.drain()
                        v.tensor_tensor(ob2[:], ob2[:], bnA[:], ALU.mult)
                        v.drain()
                        v.tensor_scalar(ob2[:], ob2[:], -0.5, 1.5, ALU.mult, ALU.add)
                        v.drain()
                        v.tensor_tensor(qbs2[:], qbs2[:], ob2[:], ALU.mult)
                        v.drain()
                        v.tensor_tensor(qbs2[:], qbs2[:], bnG[:], ALU.mult)
                        v.drain()
                        v.tensor_tensor(bnM[:], bnM[:], qbs2[:], ALU.mult)
                        v.drain()
                        v.tensor_tensor(bnB[:], bnB[:], bnM[:], ALU.subtract)
                        v.drain()
                        v.tensor_single_scalar(bnA[:], qbs2[:], 0.25, ALU.mult)
                        v.drain()

                with nc.Block() as blk:
                    ex0 = cc.get("ex")
                    ev0 = cc.get("ev")

                    @blk.vector
                    def _(v, ev0=ev0, ex0=ex0):
                        for c in range(CH):
                            if c >= 1:
                                v.wait_ge(sEx, ex0 + c)
                            v.tensor_tensor(ocA[:, 0:512], XN[:, c, 0:T:4], XN[:, c, 1:T:4], ALU.add)
                            v.drain()
                            v.tensor_tensor(ocA[:, 0:512], ocA[:, 0:512], XN[:, c, 2:T:4], ALU.add)
                            v.drain()
                            v.tensor_tensor(ocA[:, 0:512], ocA[:, 0:512], XN[:, c, 3:T:4], ALU.add)
                            v.drain().then_inc(sEv, 1)

                    @blk.scalar
                    def _(s, ev0=ev0):
                        for c in range(CH):
                            s.wait_ge(sEv, ev0 + c + 1)
                            s.activation(XR[:, c, :], ocA[:, 0:512], AF.Identity,
                                         bias=bnB[:, c:c + 1], scale=bnA[:, c:c + 1]).then_inc(sEx, 1)
                    cc.inc("ev", CH)
                    cc.inc("ex", CH)

                for c in range(CH):
                    with nc.Block() as blk:
                        @blk.sync
                        def _(sync, c=c):
                            sync.dma_start(out=Wa[:], in_=colslice_ap(wkT, c * 128)).then_inc(sLd, 16)
                            cc.inc("ld", 16)
                            sync.dma_start(out=Wb[:], in_=colslice_ap(wvT, c * 128)).then_inc(sLd, 16)
                            cc.inc("ld", 16)
                            sync.dma_start(out=WpS[:], in_=bass.AP(wpT.tensor, c * 128 * C, [[C, 128], [1, C]]))\
                                .then_inc(sLd, 16)
                            cc.inc("ld", 16)

                    with nc.Block() as blk:
                        mm0, ex0 = cc.get("mm"), cc.get("ex")

                        @blk.tensor
                        def _(t, c=c, ex0=ex0):
                            t.wait_ge(sLd, cc.get("ld"))
                            for tt in range(4):
                                if tt >= 2:
                                    t.wait_ge(sEx, ex0 + tt - 1)
                                bank = (pA0, pA1)[tt % 2]
                                for ic in range(CH):
                                    t.matmul(bank[:, 0:128], XR[:, ic, tt * 128:(tt + 1) * 128],
                                             Wa[:, ic, :], start=(ic == 0), stop=False,
                                             skip_group_check=True)
                                for ic in range(CH):
                                    ins = t.matmul(bank[:, 128:256], XR[:, ic, tt * 128:(tt + 1) * 128],
                                                   Wb[:, ic, :], start=(ic == 0), stop=(ic == CH - 1),
                                                   skip_group_check=True)
                                ins.then_inc(sMm, 1)

                        @blk.scalar
                        def _(s, c=c, mm0=mm0):
                            for tt in range(4):
                                s.wait_ge(sMm, mm0 + tt + 1)
                                bank = (pA0, pA1)[tt % 2]
                                s.copy(ktc[:, tt, :], bank[:, 0:128])
                                s.copy(vtc[:, tt, :], bank[:, 128:256])
                                s.drain().then_inc(sEx, 1)
                        cc.inc("mm", 4)
                        cc.inc("ex", 4)

                    with nc.Block() as blk:
                        @blk.tensor
                        def _(t, c=c):
                            for h in range(4):
                                for mc in range(4):
                                    t.matmul(pE[32 * h:32 * h + 32, 0:32],
                                             ktc[:, mc, 32 * h:32 * h + 32],
                                             vtc[:, mc, 32 * h:32 * h + 32],
                                             start=(mc == 0), stop=(mc == 3),
                                             tile_position=(0, 32 * h), skip_group_check=True)

                        @blk.vector
                        def _(v, c=c):
                            v.memset(Gbd[:], 0.0)

                        @blk.sync
                        def _(sync, c=c):
                            sync.dma_start(out=Wa[:], in_=colslice_ap(wqT, c * 128)).then_inc(sLd, 16)
                            cc.inc("ld", 16)
                    with nc.Block() as blk:
                        @blk.vector
                        def _(v, c=c):
                            for h in range(4):
                                v.tensor_copy(Gbd[32 * h:32 * h + 32, 32 * h:32 * h + 32],
                                              pE[32 * h:32 * h + 32, 0:32])

                    with nc.Block() as blk:
                        mm0, ex0 = cc.get("mm"), cc.get("ex")

                        @blk.tensor
                        def _(t, c=c, ex0=ex0):
                            t.wait_ge(sLd, cc.get("ld"))
                            for tb in range(TB):
                                if tb >= 2:
                                    t.wait_ge(sEx, ex0 + tb - 1)
                                bank = (pA0, pA1)[tb % 2]
                                for ic in range(CH):
                                    ins = t.matmul(bank[:, :], Wa[:, ic, :], XN[:, ic, tb * 512:(tb + 1) * 512],
                                                   start=(ic == 0), stop=(ic == CH - 1))
                                ins.then_inc(sMm, 1)

                        @blk.scalar
                        def _(s, c=c, mm0=mm0):
                            for tb in range(TB):
                                s.wait_ge(sMm, mm0 + tb + 1)
                                bank = (pA0, pA1)[tb % 2]
                                s.activation(qaA[:, tb * 512:(tb + 1) * 512], bank[:, :], AF.Identity,
                                             bias=zcol[:, :], scale=ATT_SCALE).then_inc(sEx, 1)
                        cc.inc("mm", TB)
                        cc.inc("ex", TB)

                    with nc.Block() as blk:
                        mm0, ex0 = cc.get("mm"), cc.get("ex")

                        @blk.tensor
                        def _(t, c=c, ex0=ex0):
                            for tb in range(TB):
                                if tb >= 2:
                                    t.wait_ge(sEx, ex0 + tb - 1)
                                bank = (pB0, pB1)[tb % 2]
                                t.matmul(bank[:, :], Gbd[:], qaA[:, tb * 512:(tb + 1) * 512],
                                         start=True, stop=True).then_inc(sMm, 1)

                        @blk.scalar
                        def _(s, c=c, mm0=mm0):
                            for tb in range(TB):
                                s.wait_ge(sMm, mm0 + tb + 1)
                                bank = (pB0, pB1)[tb % 2]
                                s.copy(ocA[:, tb * 512:(tb + 1) * 512], bank[:, :])
                                s.drain().then_inc(sEx, 1)
                        cc.inc("mm", TB)
                        cc.inc("ex", TB)

                    with nc.Block() as blk:
                        mm0, ev0 = cc.get("mm"), cc.get("ev")

                        @blk.tensor
                        def _(t, c=c, ev0=ev0):
                            k = 0
                            for tile in range(CH):
                                for tb in range(TB):
                                    if k >= 2:
                                        t.wait_ge(sEv, ev0 + k - 1)
                                    bank = (pA0, pA1)[k % 2]
                                    t.matmul(bank[:, :], WpS[:, tile * 128:(tile + 1) * 128],
                                             ocA[:, tb * 512:(tb + 1) * 512], start=True, stop=True).then_inc(sMm, 1)
                                    k += 1

                        @blk.vector
                        def _(v, c=c, mm0=mm0):
                            k = 0
                            for tile in range(CH):
                                for tb in range(TB):
                                    v.wait_ge(sMm, mm0 + k + 1)
                                    bank = (pA0, pA1)[k % 2]
                                    v.tensor_tensor(X[:, tile, tb * 512:(tb + 1) * 512],
                                                    X[:, tile, tb * 512:(tb + 1) * 512],
                                                    bank[:, :], ALU.add).then_inc(sEv, 1)
                                    k += 1
                        cc.inc("mm", CH * TB)
                        cc.inc("ev", CH * TB)

        if n_experts:
            with ExitStack() as ph:
                Wea = ph.enter_context(nc.sbuf_tensor("Wea", [128, CH, 128], F32))
                Web = ph.enter_context(nc.sbuf_tensor("Web", [128, CH, 128], F32))
                WO = ph.enter_context(nc.sbuf_tensor("WO", [128, C], F32))
                qae = ph.enter_context(nc.sbuf_tensor("qae", [128, T], F32))
                kae = ph.enter_context(nc.sbuf_tensor("kae", [128, T], F32))
                va = ph.enter_context(nc.sbuf_tensor("va", [128, 16, 130], F32))
                P = ph.enter_context(nc.sbuf_tensor("P", [128, 2, 2, 512], F32))
                stg = ph.enter_context(nc.sbuf_tensor("stg", [128, 512], F32))
                oche = ph.enter_context(nc.sbuf_tensor("oche", [128, T], F32))

                with nc.Block() as blk:
                    @blk.vector
                    def _(v):
                        v.memset(va[:, :, 64:65], 1.0)
                        v.memset(va[:, :, 129:130], 1.0)

                for e in range(n_experts):
                    with nc.Block() as blk:
                        @blk.gpsimd
                        def _(g, e=e):
                            g.dma_start(out=grow[:, :], in_=gT8[e:e + 1, :]).then_inc(sBc, 16)
                            cc.inc("bc", 16)

                        @blk.sync
                        def _(sync, e=e):
                            for dst, src in ((qbS, qbD), (kbS, kbD), (vbS, vbD), (obS, obD)):
                                sync.dma_start(out=dst[:], in_=src[e, :, :]).then_inc(sLd, 16)
                                cc.inc("ld", 16)

                    with nc.Block() as blk:
                        @blk.vector
                        def _(v, e=e):
                            v.wait_ge(sLd, cc.get("ld"))
                            v.wait_ge(sBc, cc.get("bc"))
                            v.tensor_single_scalar(qbs2[:], qbS[:], FFN_SCALE, ALU.mult)

                    with nc.Block() as blk:
                        mm0, ex0, pv0, ev0, bc0 = (cc.get("mm"), cc.get("ex"), cc.get("pv"),
                                                   cc.get("ev"), cc.get("bc"))
                        wa0, wb0, wo0 = cc.get("wa"), cc.get("wb"), cc.get("wo")

                        @blk.sync
                        def _(sync, e=e, mm0=mm0):
                            for hp in range(NHP):
                                off = e * C * C + hp * 128
                                mmB = mm0 + 120 * hp
                                if hp > 0:
                                    sync.wait_ge(sMm, mm0 + 120 * (hp - 1) + 24)
                                sync.dma_start(out=Wea[:], in_=bass.AP(qWT.tensor, off, [[C, 128], [C * 128, CH], [1, 128]]))\
                                    .then_inc(sWa, 16)
                                sync.dma_start(out=Web[:], in_=bass.AP(kWT.tensor, off, [[C, 128], [C * 128, CH], [1, 128]]))\
                                    .then_inc(sWb, 16)
                                sync.dma_start(out=vbRow[:, :], in_=vbR[e:e + 1, hp * 128:(hp + 1) * 128])\
                                    .then_inc(sWo, 16)
                                if hp > 0:
                                    sync.wait_ge(sMm, mm0 + 120 * hp)
                                sync.dma_start(out=WO[:], in_=bass.AP(oWT.tensor, e * C * C + hp * 128 * C, [[C, 128], [1, C]]))\
                                    .then_inc(sWo, 16)
                                sync.wait_ge(sMm, mmB + 4)
                                sync.dma_start(out=Wea[:], in_=bass.AP(vWT.tensor, off, [[C, 128], [C * 128, CH], [1, 128]]))\
                                    .then_inc(sWa, 16)

                        @blk.tensor
                        def _(t, e=e, mm0=mm0, ex0=ex0, pv0=pv0, ev0=ev0, bc0=bc0,
                              wa0=wa0, wb0=wb0, wo0=wo0):
                            for hp in range(NHP):
                                mmB = mm0 + 120 * hp
                                exB = ex0 + 152 * hp
                                pvB = pv0 + 64 * hp
                                evB = ev0 + 44 * hp
                                bcB = bc0 + 192 * hp
                                X0 = exB + 24
                                t.wait_ge(sEv, ev0 + 44 * hp)
                                t.wait_ge(sWa, wa0 + 32 * hp + 16)
                                for g8 in range(2 * TB):
                                    if g8 == TB:
                                        t.wait_ge(sWb, wb0 + 16 * hp + 16)
                                    if g8 >= 2:
                                        t.wait_ge(sEx, exB + g8 - 1)
                                    W = Wea if g8 < TB else Web
                                    tb = g8 % TB
                                    bank = (pA0, pA1)[g8 % 2]
                                    for ic in range(CH):
                                        ins = t.matmul(bank[:, :], W[:, ic, :],
                                                       XN[:, ic, tb * 512:(tb + 1) * 512],
                                                       start=(ic == 0), stop=(ic == CH - 1))
                                    ins.then_inc(sMm, 1)
                                t.wait_ge(sWa, wa0 + 32 * hp + 32)
                                t.wait_ge(sWo, wo0 + 32 * hp + 32)
                                for tt in range(16):
                                    t.wait_ge(sEx, exB + 2 * TB + tt - 1)
                                    bank = (pA0, pA1)[tt % 2]
                                    for ic in range(CH):
                                        t.matmul(bank[:, 0:128],
                                                 XN[:, ic, tt * 128:(tt + 1) * 128],
                                                 Wea[:, ic, :],
                                                 start=(ic == 0), stop=False,
                                                 skip_group_check=True)
                                    t.matmul(bank[:, 0:128], onesr[0:1, :],
                                             vbRow[0:1, 0:128],
                                             start=False, stop=True,
                                             skip_group_check=True).then_inc(sMm, 1)
                                for qb_ in range(TB):
                                    if qb_ > 0:
                                        t.wait_ge(sEv, evB + 3 * qb_ - 1)
                                    for kc in range(KC):
                                        t.wait_ge(sEx, X0 + 32 * qb_ + 2 * kc)
                                        t.matmul(pB0[:, :], kae[0:64, kc * 128:(kc + 1) * 128],
                                                 qae[0:64, qb_ * 512:(qb_ + 1) * 512],
                                                 start=True, stop=True, skip_group_check=True)
                                        t.matmul(pB1[:, :], kae[64:128, kc * 128:(kc + 1) * 128],
                                                 qae[64:128, qb_ * 512:(qb_ + 1) * 512],
                                                 start=True, stop=True, skip_group_check=True).then_inc(sMm, 1)
                                        if kc > 0:
                                            t.matmul(pC0[:, :], va[:, kc - 1, 0:65],
                                                     P[:, 0, (kc - 1) % 2, :],
                                                     start=(kc - 1 == 0), stop=False,
                                                     skip_group_check=True)
                                            t.matmul(pC1[:, :], va[:, kc - 1, 65:130],
                                                     P[:, 1, (kc - 1) % 2, :],
                                                     start=(kc - 1 == 0), stop=False,
                                                     skip_group_check=True).then_inc(sPv, 1)
                                    t.wait_ge(sEx, X0 + 32 * qb_ + 32)
                                    t.matmul(pC0[:, :], va[:, KC - 1, 0:65], P[:, 0, (KC - 1) % 2, :],
                                             start=False, stop=True, skip_group_check=True)
                                    t.matmul(pC1[:, :], va[:, KC - 1, 65:130], P[:, 1, (KC - 1) % 2, :],
                                             start=False, stop=True, skip_group_check=True).then_inc(sPv, 1)
                                t.wait_ge(sEv, evB + 12)
                                t.wait_ge(sBc, bcB + 192)
                                k = 0
                                for tile in range(CH):
                                    for tb in range(TB):
                                        if k >= 2:
                                            t.wait_ge(sEv, evB + 12 + k - 1)
                                        bank = (pA0, pA1)[k % 2]
                                        t.matmul(bank[:, :], WO[:, tile * 128:(tile + 1) * 128],
                                                 oche[:, tb * 512:(tb + 1) * 512],
                                                 start=True, stop=True).then_inc(sMm, 1)
                                        k += 1

                        @blk.scalar
                        def _(s, e=e, mm0=mm0, ex0=ex0, pv0=pv0, bc0=bc0):
                            for hp in range(NHP):
                                mmB = mm0 + 120 * hp
                                exB = ex0 + 152 * hp
                                pvB = pv0 + 64 * hp
                                bcB = bc0 + 192 * hp
                                for g8 in range(2 * TB):
                                    s.wait_ge(sMm, mmB + g8 + 1)
                                    tb = g8 % TB
                                    bank = (pA0, pA1)[g8 % 2]
                                    dst, bias, scale = ((qae, qbs2, FFN_SCALE) if g8 < TB
                                                        else (kae, kbS, 1.0))
                                    s.activation(dst[:, tb * 512:(tb + 1) * 512], bank[:, :], AF.Identity,
                                                 bias=bias[:, hp:hp + 1], scale=scale).then_inc(sEx, 1)
                                for tt in range(16):
                                    s.wait_ge(sMm, mmB + 2 * TB + tt + 1)
                                    bank = (pA0, pA1)[tt % 2]
                                    s.copy(va[:, tt, 0:64], bank[:, 0:64])
                                    s.copy(va[:, tt, 65:129], bank[:, 64:128]).then_inc(sEx, 1)
                                for qb_ in range(TB):
                                    s.wait_ge(sBc, bcB + 48 * qb_)
                                    for kc in range(KC):
                                        s.wait_ge(sMm, mmB + 24 + 16 * qb_ + kc + 1)
                                        if kc >= 2:
                                            s.wait_ge(sPv, pvB + 16 * qb_ + kc - 1)
                                        s.activation(P[:, 0, kc % 2, :], pB0[:, :], AF.Exp,
                                                     bias=zcol[:, :], scale=1.0).then_inc(sEx, 1)
                                        s.activation(P[:, 1, kc % 2, :], pB1[:, :], AF.Exp,
                                                     bias=zcol[:, :], scale=1.0).then_inc(sEx, 1)

                        @blk.vector
                        def _(v, e=e, mm0=mm0, pv0=pv0, ev0=ev0, bc0=bc0):
                            for hp in range(NHP):
                                mmB = mm0 + 120 * hp
                                pvB = pv0 + 64 * hp
                                evB = ev0 + 44 * hp
                                bcB = bc0 + 192 * hp
                                for qb_ in range(TB):
                                    sl = slice(qb_ * 512, (qb_ + 1) * 512)
                                    v.wait_ge(sPv, pvB + 16 * (qb_ + 1))
                                    v.reciprocal(rrow[0:1, 0, :], pC0[64:65, :])
                                    v.reciprocal(rrow[0:1, 1, :], pC1[64:65, :]).then_inc(sEv, 1)
                                    v.wait_ge(sBc, bcB + 48 * qb_ + 32)
                                    v.tensor_tensor(oche[0:64, sl], pC0[0:64, :], stg[0:64, :], ALU.mult)
                                    v.tensor_copy(P[0:64, 0, 0, :], pC1[0:64, :]).then_inc(sEv, 1)
                                    v.wait_ge(sBc, bcB + 48 * qb_ + 48)
                                    v.tensor_tensor(oche[64:128, sl], oche[64:128, sl],
                                                    stg[64:128, :], ALU.mult).then_inc(sEv, 1)
                                k = 0
                                for tile in range(CH):
                                    for tb in range(TB):
                                        v.wait_ge(sMm, mmB + 88 + k + 1)
                                        bank = (pA0, pA1)[k % 2]
                                        v.tensor_tensor(X[:, tile, tb * 512:(tb + 1) * 512],
                                                        X[:, tile, tb * 512:(tb + 1) * 512],
                                                        bank[:, :], ALU.add).then_inc(sEv, 1)
                                        k += 1

                        @blk.gpsimd
                        def _(g, e=e, ev0=ev0, bc0=bc0):
                            for hp in range(NHP):
                                evB = ev0 + 44 * hp
                                for qb_ in range(TB):
                                    sl = slice(qb_ * 512, (qb_ + 1) * 512)
                                    g.wait_ge(sEv, evB + 3 * qb_ + 1)
                                    g.tensor_tensor(rrow[0:1, 0, :], rrow[0:1, 0, :], grow[0:1, sl], ALU.mult)
                                    g.tensor_tensor(rrow[0:1, 1, :], rrow[0:1, 1, :], grow[0:1, sl], ALU.mult)
                                    g.drain()
                                    g.dma_start(out=stg[0:64, :],
                                                in_=bass.AP(rrow, 0, [[1024, 1], [0, 64], [1, 512]]))\
                                        .then_inc(sBc, 16)
                                    g.dma_start(out=stg[64:128, :],
                                                in_=bass.AP(rrow, 512, [[1024, 1], [0, 64], [1, 512]]))\
                                        .then_inc(sBc, 16)
                                    g.wait_ge(sEv, evB + 3 * qb_ + 2)
                                    g.dma_start(out=oche[64:128, sl], in_=P[0:64, 0, 0, :])\
                                        .then_inc(sBc, 16)
                        cc.inc("mm", 120 * NHP)
                        cc.inc("ex", 152 * NHP)
                        cc.inc("pv", 64 * NHP)
                        cc.inc("ev", 44 * NHP)
                        cc.inc("bc", 192 * NHP)
                        cc.inc("wa", 32 * NHP)
                        cc.inc("wb", 16 * NHP)
                        cc.inc("wo", 32 * NHP)

                    with nc.Block() as blk:
                        @blk.vector
                        def _(v, e=e):
                            v.tensor_copy(ob2[:], obS[:])
                    with nc.Block() as blk:
                        bc0 = cc.get("bc")

                        @blk.gpsimd
                        def _(g, e=e):
                            for tb in range(TB):
                                g.dma_start(out=stg[:, :],
                                            in_=bass.AP(grow, tb * 512, [[T, 1], [0, 128], [1, 512]]))\
                                    .then_inc(sBc, 16)
                                g.wait_ge(sEv, cc.get("ev") + tb + 1)

                        @blk.vector
                        def _(v, e=e, bc0=bc0):
                            for tb in range(TB):
                                v.wait_ge(sBc, bc0 + 16 * (tb + 1))
                                for tile in range(CH):
                                    ins = v.scalar_tensor_tensor(X[:, tile, tb * 512:(tb + 1) * 512],
                                                                 stg[:, :], ob2[:, tile:tile + 1],
                                                                 X[:, tile, tb * 512:(tb + 1) * 512],
                                                                 ALU.mult, ALU.add)
                                ins.then_inc(sEv, 1)
                        cc.inc("bc", 16 * TB)
                        cc.inc("ev", TB)

        with nc.Block() as blk:
            @blk.sync
            def _(sync):
                nst = 16
                sync.dma_start(out=bass.AP(yT.tensor, 0, [[T, 128], [T * 128, CH], [1, T]]),
                               in_=X[:]).then_inc(sSt, 16)
                if dbg_gates:
                    sync.dma_start(out=gdbg[:, :], in_=gT8[:, :]).then_inc(sSt, 16)
                    sync.dma_start(out=ldbg[:, :, :], in_=lgS[:, :, :]).then_inc(sSt, 16)
                    sync.dma_start(out=g16dbg[:, :, :], in_=g16[:, :, :]).then_inc(sSt, 16)
                    nst += 48
                sync.wait_ge(sSt, nst)

    return nc


def kernel(**inputs):
    I = {k: np.ascontiguousarray(np.asarray(v), dtype=np.float32) for k, v in inputs.items()}
    x = I["L0_attn_x"]
    cT = lambda a: np.ascontiguousarray(a.T)
    chunk1 = lambda a: np.ascontiguousarray(a.reshape(-1, 128).T)
    chunk2 = lambda a: np.ascontiguousarray(a.reshape(NE, -1, 128).transpose(0, 2, 1))
    shared = dict(
        wqT=cT(I["L0_attn_Wq"]), wkT=cT(I["L0_attn_Wk"]),
        wvT=cT(I["L0_attn_Wv"]), wpT=cT(I["L0_attn_Wp"]),
        bp=chunk1(I["L0_attn_bp"]), bn_mean=chunk1(I["L0_attn_bn_mean"]),
        bn_var=chunk1(I["L0_attn_bn_var"]), bn_g=chunk1(I["L0_attn_bn_gamma"]),
        bn_b=chunk1(I["L0_attn_bn_beta"]),
        rw1T=cT(I["L0_ffn_rW1"]), rb1=chunk1(I["L0_ffn_rb1"]),
        lng=chunk1(I["L0_ffn_ln_g"]), lnb=chunk1(I["L0_ffn_ln_b"]),
        rw2T=np.ascontiguousarray(cT(I["L0_ffn_rW2"]).reshape(HC4, 128, NE).transpose(1, 0, 2)),
        rb2=I["L0_ffn_rb2"].reshape(1, NE),
        qWT=np.ascontiguousarray(I["L0_ffn_qW"].transpose(0, 2, 1)),
        kWT=np.ascontiguousarray(I["L0_ffn_kW"].transpose(0, 2, 1)),
        vWT=np.ascontiguousarray(I["L0_ffn_vW"].transpose(0, 2, 1)),
        oWT=np.ascontiguousarray(I["L0_ffn_oW"].transpose(0, 2, 1)),
        qb=chunk2(I["L0_ffn_qb"]), kb=chunk2(I["L0_ffn_kb"]),
        vb=chunk2(I["L0_ffn_vb"]), ob=chunk2(I["L0_ffn_ob"]),
        vbR=I["L0_ffn_vb"],
        ident=np.eye(128, dtype=np.float32),
    )
    nc = build_nc()
    in_maps = [dict(shared, xT=np.ascontiguousarray(x[b].T)) for b in range(8)]
    res = run_bass_kernel_spmd(nc, in_maps, core_ids=list(range(8)))
    y = np.stack([np.asarray(r["yT"]).T for r in res.results])
    return np.ascontiguousarray(y).astype(np.float32)


# version 12
# speedup vs baseline: 1.1898x; 1.1898x over previous
import numpy as np
import concourse.bass as bass
import concourse.mybir as mybir
from concourse.bass_utils import run_bass_kernel_spmd
from contextlib import ExitStack

F32 = mybir.dt.float32
AF = mybir.ActivationFunctionType
ALU = mybir.AluOpType
AXX = mybir.AxisListType.X

T = 2048
C = 1024
CH = 8
NE = 8
NHP = 8
TB = 4
KC = 16
HD = 512
HC4 = HD // 128
ATT_SCALE = float(1.0 / np.sqrt(32.0))
FFN_SCALE = 0.125


class Ctr:
    def __init__(self):
        self.v = {}

    def inc(self, k, n=1):
        self.v[k] = self.v.get(k, 0) + n
        return self.v[k]

    def get(self, k):
        return self.v.get(k, 0)


def build_nc(do_header=4, do_attn=True, n_experts=NE, dbg_gates=False):
    do_header = int(do_header)
    nc = bass.Bass("TRN2", target_bir_lowering=False)
    din = lambda n, s: nc.dram_tensor(n, s, F32, kind="ExternalInput").ap()

    xT = din("xT", [C, T])
    wqT, wkT, wvT, wpT = (din(n, [C, C]) for n in ("wqT", "wkT", "wvT", "wpT"))
    bp = din("bp", [128, CH])
    bn_mean, bn_var, bn_g, bn_b = (din(n, [128, CH]) for n in ("bn_mean", "bn_var", "bn_g", "bn_b"))
    rw1T = din("rw1T", [C, HD])
    rb1 = din("rb1", [128, HC4])
    lng, lnb = din("lng", [128, HC4]), din("lnb", [128, HC4])
    rw2T = din("rw2T", [128, HC4, NE])
    rb2 = din("rb2", [1, NE])
    qWT, kWT, vWT, oWT = (din(n, [NE, C, C]) for n in ("qWT", "kWT", "vWT", "oWT"))
    qbD, kbD, vbD, obD = (din(n, [NE, 128, CH]) for n in ("qb", "kb", "vb", "ob"))
    vbR = din("vbR", [NE, C])
    identD = din("ident", [128, 128])
    yT = nc.dram_tensor("yT", [C, T], F32, kind="ExternalOutput").ap()
    gdbg = nc.dram_tensor("gdbg", [NE, T], F32, kind="ExternalOutput").ap() if dbg_gates else None
    ldbg = nc.dram_tensor("ldbg", [128, 16, NE], F32, kind="ExternalOutput").ap() if dbg_gates else None
    hdbg = nc.dram_tensor("hdbg", [128, T], F32, kind="ExternalOutput").ap() if dbg_gates else None
    sdbg = nc.dram_tensor("sdbg", [128, T], F32, kind="ExternalOutput").ap() if dbg_gates else None
    bdbg = nc.dram_tensor("bdbg", [128, T], F32, kind="ExternalOutput").ap() if dbg_gates else None
    xndbg = nc.dram_tensor("xndbg", [128, T], F32, kind="ExternalOutput").ap() if dbg_gates else None
    hrdbg = nc.dram_tensor("hrdbg", [128, T], F32, kind="ExternalOutput").ap() if dbg_gates else None
    g16dbg = nc.dram_tensor("g16dbg", [128, 16, NE], F32, kind="ExternalOutput").ap() if dbg_gates else None
    hscr = nc.dram_tensor("hscr", [HD, T], F32, kind="Internal").ap()

    cc = Ctr()

    def vslice_ap(t, e_off=0, n=C):
        return bass.AP(t.tensor, e_off, [[n // 128, 128], [1, n // 128]])

    def colslice_ap(w, base_elem, nin=C):
        return bass.AP(w.tensor, base_elem, [[w.tensor.shape[-1], 128],
                                             [w.tensor.shape[-1] * 128, nin // 128],
                                             [1, 128]])

    with ExitStack() as ctx:
        sb = lambda n, s, d=F32: ctx.enter_context(nc.sbuf_tensor(n, s, d))
        ps = lambda n, s: ctx.enter_context(nc.psum_tensor(n, s, F32))
        sem = lambda n: ctx.enter_context(nc.semaphore(n))

        X = sb("X", [128, CH, T])
        XN = sb("XN", [128, CH, T])
        g16 = sb("g16", [128, 16, NE])
        grow = sb("grow", [1, T])
        rrow = sb("rrow", [1, 2, 512])
        onesc = sb("onesc", [128, 1])
        onesr = sb("onesr", [1, 128])
        vbRow = sb("vbRow", [1, 128])
        zcol = sb("zcol", [128, 1])
        bpS = sb("bpS", [128, CH])
        bnA = sb("bnA", [128, CH])
        bnB = sb("bnB", [128, CH])
        bnM = sb("bnM", [128, CH])
        bnG = sb("bnG", [128, CH])
        rb1S = sb("rb1S", [128, HC4])
        lngS = sb("lngS", [128, HC4])
        lnbS = sb("lnbS", [128, HC4])
        r2S = sb("r2S", [128, HC4, NE])
        rb2r = sb("rb2r", [128, NE])
        qbS = sb("qbS", [128, CH])
        kbS = sb("kbS", [128, CH])
        vbS = sb("vbS", [128, CH])
        obS = sb("obS", [128, CH])
        qbs2 = sb("qbs2", [128, CH])
        ob2 = sb("ob2", [128, CH])
        lgS = sb("lgS", [128, 16, NE])
        gmA = sb("gmA", [128, 16])
        gmB = sb("gmB", [128, 16])
        gw1 = sb("gw1", [128, NE])

        pA0 = ps("pA0", [128, 512])
        pA1 = ps("pA1", [128, 512])
        pB0 = ps("pB0", [128, 512])
        pB1 = ps("pB1", [128, 512])
        pC0 = ps("pC0", [65, 512])
        pC1 = ps("pC1", [65, 512])
        pD = ps("pD", [128, 512])
        pE = ps("pE", [128, 512])

        sLd = sem("sLd")
        sBc = sem("sBc")
        sMm = sem("sMm")
        sEx = sem("sEx")
        sEv = sem("sEv")
        sPv = sem("sPv")
        sSt = sem("sSt")
        sWt = sem("sWt")
        sWa = sem("sWa")
        sWb = sem("sWb")
        sWo = sem("sWo")
        sStg = sem("sStg")

        with nc.Block() as blk:
            @blk.sync
            def _(sync):
                sync.dma_start(out=X[:], in_=bass.AP(xT.tensor, 0, [[T, 128], [T * 128, CH], [1, T]]))\
                    .then_inc(sLd, 16); cc.inc("ld", 16)
                for dst, src in ((bpS, bp), (bnA, bn_var), (bnB, bn_b), (bnM, bn_mean), (bnG, bn_g)):
                    sync.dma_start(out=dst[:], in_=vslice_ap(src)).then_inc(sLd, 16); cc.inc("ld", 16)
                for dst, src in ((rb1S, rb1), (lngS, lng), (lnbS, lnb)):
                    sync.dma_start(out=dst[:], in_=vslice_ap(src, 0, HD)).then_inc(sLd, 16); cc.inc("ld", 16)
                sync.dma_start(out=r2S[:], in_=rw2T[:, :, :]).then_inc(sLd, 16); cc.inc("ld", 16)
                sync.dma_start(out=rb2r[0:1, :], in_=rb2[:, :]).then_inc(sLd, 16); cc.inc("ld", 16)

            @blk.vector
            def _(v):
                v.memset(onesc[:], 1.0)
                v.memset(onesr[:], 1.0)
                v.memset(zcol[:], 0.0)

            @blk.gpsimd
            def _(g):
                g.wait_ge(sLd, cc.get("ld"))
                g.dma_start(out=rb2r[1:128, :], in_=bass.AP(rb2r, 0, [[NE, 1], [0, 127], [1, NE]]))\
                    .then_inc(sBc, 16); cc.inc("bc", 16)

        with ExitStack() as ph:
            S_rep = ph.enter_context(nc.sbuf_tensor("S_rep", [128, T], F32))
            B_rep = ph.enter_context(nc.sbuf_tensor("B_rep", [128, T], F32))
            rowsW = ph.enter_context(nc.sbuf_tensor("rowsW", [128, 4, 512], F32))
            gT8 = ph.enter_context(nc.sbuf_tensor("gT8", [NE, T], F32))
            scr4 = ph.enter_context(nc.sbuf_tensor("scr4", [128, 2, 512], F32))
            Wh = ph.enter_context(nc.sbuf_tensor("Wh", [128, CH, 128], F32))
            happ = ph.enter_context(nc.sbuf_tensor("happ", [128, T], F32))
            idS = ph.enter_context(nc.sbuf_tensor("idS", [128, 128], F32))

            with nc.Block() as blk:
                @blk.sync
                def _(sync):
                    sync.dma_start(out=idS[:], in_=identD[:, :]).then_inc(sLd, 16); cc.inc("ld", 16)

            with nc.Block() as blk:
                @blk.tensor
                def _(t):
                    t.wait_ge(sLd, cc.get("ld"))
                    for j in range(4):
                        for c in range(CH):
                            t.matmul(pB0[32 * j:32 * j + 1, :], onesc[:],
                                     X[:, c, j * 512:(j + 1) * 512],
                                     start=(c == 0), stop=(c == CH - 1),
                                     tile_position=(0, 32 * j), skip_group_check=True)

            with nc.Block() as blk:
                mm0, ex0 = cc.get("mm"), cc.get("ex")

                @blk.scalar
                def _(s):
                    k = 0
                    for c in range(CH):
                        for q4 in range(4):
                            if k >= 2:
                                s.wait_ge(sMm, mm0 + k - 1)
                            s.activation(scr4[:, k % 2, :], X[:, c, q4 * 512:(q4 + 1) * 512], AF.Square)
                            s.drain().then_inc(sEx, 1)
                            k += 1

                @blk.tensor
                def _(t):
                    k = 0
                    for c in range(CH):
                        for q4 in range(4):
                            t.wait_ge(sEx, ex0 + k + 1)
                            t.matmul(pB1[32 * q4:32 * q4 + 1, :], onesc[:], scr4[:, k % 2, :],
                                     start=(c == 0), stop=(c == CH - 1),
                                     tile_position=(0, 32 * q4), skip_group_check=True).then_inc(sMm, 1)
                            k += 1
                cc.inc("ex", 32)
                cc.inc("mm", 32)

            def ln_rows_and_bcast(px, pxx, inv, eps):
                with nc.Block() as blk:
                    @blk.vector
                    def _(v):
                        P4 = [slice(32 * j, 32 * j + 1) for j in range(4)]
                        for p in P4:
                            v.tensor_single_scalar(rowsW[p, 0, :], px[p, :], inv, ALU.mult)
                            v.tensor_single_scalar(rowsW[p, 1, :], pxx[p, :], inv, ALU.mult)
                        v.drain()
                        for p in P4:
                            v.tensor_tensor(rowsW[p, 3, :], rowsW[p, 0, :], rowsW[p, 0, :], ALU.mult)
                        v.drain()
                        for p in P4:
                            v.tensor_tensor(rowsW[p, 1, :], rowsW[p, 1, :], rowsW[p, 3, :], ALU.subtract)
                        v.drain()
                        for p in P4:
                            v.tensor_single_scalar(rowsW[p, 1, :], rowsW[p, 1, :], eps, ALU.add)
                with nc.Block() as blk:
                    @blk.scalar
                    def _(s):
                        for j in range(4):
                            p = slice(32 * j, 32 * j + 1)
                            s.activation(rowsW[p, 2, :], rowsW[p, 1, :], AF.Sqrt)
                with nc.Block() as blk:
                    @blk.vector
                    def _(v):
                        P4 = [slice(32 * j, 32 * j + 1) for j in range(4)]
                        for p in P4:
                            v.reciprocal(rowsW[p, 2, :], rowsW[p, 2, :])
                        v.drain()
                        for p in P4:
                            v.tensor_tensor(rowsW[p, 3, :], rowsW[p, 2, :], rowsW[p, 2, :], ALU.mult)
                        v.drain()
                        for p in P4:
                            v.tensor_tensor(rowsW[p, 3, :], rowsW[p, 3, :], rowsW[p, 1, :], ALU.mult)
                        v.drain()
                        for p in P4:
                            v.tensor_scalar(rowsW[p, 3, :], rowsW[p, 3, :], -0.5, 1.5, ALU.mult, ALU.add)
                        v.drain()
                        for p in P4:
                            v.tensor_tensor(rowsW[p, 2, :], rowsW[p, 2, :], rowsW[p, 3, :], ALU.mult)
                        v.drain()
                        for p in P4:
                            v.scalar_tensor_tensor(rowsW[p, 3, :], rowsW[p, 0, :], -1.0,
                                                   rowsW[p, 2, :], ALU.mult, ALU.mult)
                with nc.Block() as blk:
                    @blk.gpsimd
                    def _(g):
                        g.wait_ge(sStg, cc.get("stg"))
                        for j in range(4):
                            sl = slice(j * 512, (j + 1) * 512)
                            g.dma_start(out=S_rep[:, sl],
                                        in_=bass.AP(rowsW, 32 * j * 2048 + 2 * 512, [[2048, 1], [0, 128], [1, 512]]))\
                                .then_inc(sBc, 16); cc.inc("bc", 16)
                            g.dma_start(out=B_rep[:, sl],
                                        in_=bass.AP(rowsW, 32 * j * 2048 + 3 * 512, [[2048, 1], [0, 128], [1, 512]]))\
                                .then_inc(sBc, 16); cc.inc("bc", 16)
                with nc.Block() as blk:
                    @blk.vector
                    def _(v):
                        v.wait_ge(sBc, cc.get("bc"))

            ln_rows_and_bcast(pB0, pB1, 1.0 / 1024.0, 1e-5)

            with nc.Block() as blk:
                @blk.vector
                def _(v):
                    for c in range(CH):
                        v.tensor_tensor(XN[:, c, :], X[:, c, :], S_rep[:], ALU.mult)
                    v.drain()
                    for c in range(CH):
                        v.tensor_tensor(XN[:, c, :], XN[:, c, :], B_rep[:], ALU.add)
                    v.drain()
                    for c in range(CH):
                        v.tensor_single_scalar(X[:, c, :], X[:, c, :], bpS[:, c:c + 1], ALU.add)

            stage_ld = {}
            for hc in range(HC4 if do_header >= 1 else 0):
                with nc.Block() as blk:
                    @blk.sync
                    def _(sync, hc=hc):
                        sync.dma_start(out=Wh[:], in_=colslice_ap(rw1T, hc * 128))\
                            .then_inc(sLd, 16); cc.inc("ld", 16)
                with nc.Block() as blk:
                    mm0, ex0 = cc.get("mm"), cc.get("ex")

                    @blk.tensor
                    def _(t, hc=hc, mm0=mm0, ex0=ex0):
                        t.wait_ge(sLd, cc.get("ld"))
                        if hc > 0:
                            t.wait_ge(sStg, stage_ld[hc - 1])
                        for tb in range(TB):
                            if tb >= 2:
                                t.wait_ge(sEx, ex0 + tb - 1)
                            bank = (pA0, pA1)[tb % 2]
                            for ic in range(CH):
                                t.matmul(bank[:, :], Wh[:, ic, :], XN[:, ic, tb * 512:(tb + 1) * 512],
                                         start=(ic == 0), stop=(ic == CH - 1))
                            t.drain().then_inc(sMm, 1)

                    @blk.scalar
                    def _(s, hc=hc, mm0=mm0):
                        for tb in range(TB):
                            s.wait_ge(sMm, mm0 + tb + 1)
                            bank = (pA0, pA1)[tb % 2]
                            s.activation(B_rep[:, tb * 512:(tb + 1) * 512], bank[:, :], AF.Identity,
                                         bias=rb1S[:, hc:hc + 1], scale=1.0).then_inc(sEx, 1)
                    cc.inc("mm", TB)
                    cc.inc("ex", TB)

                with nc.Block() as blk:
                    @blk.sync
                    def _(sync, hc=hc):
                        sync.dma_start(out=bass.AP(hscr.tensor, hc * 128 * T, [[T, 128], [1, T]]),
                                       in_=B_rep[:, :]).then_inc(sStg, 16)
                        cc.inc("stg", 16)
                        if dbg_gates and hc == 0:
                            sync.dma_start(out=hdbg[:, :], in_=B_rep[:, :]).then_inc(sLd, 16)
                            cc.inc("ld", 16)
                    stage_ld[hc] = cc.get("stg")

                    if do_header >= 2:
                        @blk.tensor
                        def _(t, hc=hc):
                            for j in range(4):
                                t.matmul(pB0[32 * j:32 * j + 1, :], onesc[:],
                                         B_rep[:, j * 512:(j + 1) * 512],
                                         start=(hc == 0), stop=(hc == HC4 - 1),
                                         tile_position=(0, 32 * j), skip_group_check=True)

                if do_header >= 2:
                    with nc.Block() as blk:
                        mm0, ex0 = cc.get("mm"), cc.get("ex")

                        @blk.scalar
                        def _(s, hc=hc, mm0=mm0):
                            for q4 in range(4):
                                if q4 >= 2:
                                    s.wait_ge(sMm, mm0 + q4 - 1)
                                s.activation(scr4[:, q4 % 2, :], B_rep[:, q4 * 512:(q4 + 1) * 512], AF.Square)
                                s.drain().then_inc(sEx, 1)

                        @blk.tensor
                        def _(t, hc=hc, ex0=ex0):
                            for q4 in range(4):
                                t.wait_ge(sEx, ex0 + q4 + 1)
                                t.matmul(pB1[32 * q4:32 * q4 + 1, :], onesc[:], scr4[:, q4 % 2, :],
                                         start=(hc == 0), stop=(hc == HC4 - 1),
                                         tile_position=(0, 32 * q4), skip_group_check=True)
                                t.drain().then_inc(sMm, 1)
                        cc.inc("mm", 4)
                        cc.inc("ex", 4)

            if do_header >= 3:
                ln_rows_and_bcast(pB0, pB1, 1.0 / HD, 1e-5)

            if dbg_gates and do_header >= 3:
                with nc.Block() as blk:
                    @blk.sync
                    def _(sync):
                        sync.dma_start(out=sdbg[:, :], in_=S_rep[:, :]).then_inc(sLd, 16)
                        cc.inc("ld", 16)
                        sync.dma_start(out=bdbg[:, :], in_=B_rep[:, :]).then_inc(sLd, 16)
                        cc.inc("ld", 16)
                        sync.dma_start(out=xndbg[:, :], in_=XN[:, 0, :]).then_inc(sLd, 16)
                        cc.inc("ld", 16)
            for hc in range(HC4 if do_header >= 3 else 0):
                with nc.Block() as blk:
                    @blk.sync
                    def _(sync, hc=hc):
                        sync.wait_ge(sStg, cc.get("stg"))
                        sync.dma_start(out=happ[:, :],
                                       in_=bass.AP(hscr.tensor, hc * 128 * T, [[T, 128], [1, T]]))\
                            .then_inc(sLd, 16); cc.inc("ld", 16)
                with nc.Block() as blk:
                    @blk.vector
                    def _(v, hc=hc):
                        v.wait_ge(sLd, cc.get("ld"))
                        v.tensor_tensor(happ[:, :], happ[:, :], S_rep[:, :], ALU.mult)
                        v.drain()
                        v.tensor_tensor(happ[:, :], happ[:, :], B_rep[:, :], ALU.add)
                with nc.Block() as blk:
                    @blk.scalar
                    def _(s, hc=hc):
                        s.activation(happ[:, :], happ[:, :], AF.Relu,
                                     bias=lnbS[:, hc:hc + 1], scale=lngS[:, hc:hc + 1])
                if dbg_gates and hc == 0:
                    with nc.Block() as blk:
                        @blk.sync
                        def _(sync, hc=hc):
                            sync.dma_start(out=hrdbg[:, :], in_=happ[:, :]).then_inc(sLd, 16)
                            cc.inc("ld", 16)
                with nc.Block() as blk:
                    mm0, ev0 = cc.get("mm"), cc.get("ev")

                    @blk.tensor
                    def _(t, hc=hc, ev0=ev0):
                        for tt in range(16):
                            if tt >= 2:
                                t.wait_ge(sEv, ev0 + tt - 1)
                            bank = (pD, pE)[tt % 2]
                            t.matmul(bank[:, 0:8], happ[:, tt * 128:(tt + 1) * 128],
                                     r2S[:, hc, :], start=True, stop=True,
                                     skip_group_check=True).then_inc(sMm, 1)

                    @blk.vector
                    def _(v, hc=hc, mm0=mm0):
                        for tt in range(16):
                            v.wait_ge(sMm, mm0 + tt + 1)
                            bank = (pD, pE)[tt % 2]
                            if hc == 0:
                                v.tensor_copy(lgS[:, tt, :], bank[:, 0:8])
                                v.drain().then_inc(sEv, 1)
                            else:
                                v.tensor_tensor(lgS[:, tt, :], lgS[:, tt, :], bank[:, 0:8], ALU.add)
                                v.drain().then_inc(sEv, 1)
                    cc.inc("mm", 16)
                    cc.inc("ev", 16)

            if do_header >= 4:
                with nc.Block() as blk:
                    @blk.vector
                    def _(v):
                        for tt in range(16):
                            v.tensor_tensor(lgS[:, tt, :], lgS[:, tt, :], rb2r[:, :], ALU.add)
                with nc.Block() as blk:
                    @blk.vector
                    def _(v):
                        for tt in range(16):
                            v.reduce_max(gmA[:, tt:tt + 1], lgS[:, tt, :], axis=AXX)
                with nc.Block() as blk:
                    @blk.vector
                    def _(v):
                        for tt in range(16):
                            v.tensor_tensor(g16[:, tt, :], lgS[:, tt, :],
                                            bass.AP(gmA, tt, [[16, 128], [0, NE]]), ALU.is_equal)
                        v.drain()
                        for tt in range(16):
                            v.tensor_single_scalar(g16[:, tt, :], g16[:, tt, :], -1e30, ALU.mult)
                        v.drain()
                        for tt in range(16):
                            v.tensor_tensor(g16[:, tt, :], lgS[:, tt, :], g16[:, tt, :], ALU.add)
                with nc.Block() as blk:
                    @blk.vector
                    def _(v):
                        for tt in range(16):
                            v.reduce_max(gmB[:, tt:tt + 1], g16[:, tt, :], axis=AXX)
                with nc.Block() as blk:
                    @blk.vector
                    def _(v):
                        for tt in range(16):
                            v.tensor_single_scalar(gmA[:, tt:tt + 1], gmA[:, tt:tt + 1], -1.0, ALU.mult)
                with nc.Block() as blk:
                    @blk.scalar
                    def _(s):
                        for tt in range(16):
                            s.activation(g16[:, tt, :], lgS[:, tt, :], AF.Exp,
                                         bias=gmA[:, tt:tt + 1], scale=1.0)
                with nc.Block() as blk:
                    @blk.vector
                    def _(v):
                        for tt in range(16):
                            v.tensor_tensor(lgS[:, tt, :], lgS[:, tt, :],
                                            bass.AP(gmB, tt, [[16, 128], [0, NE]]), ALU.is_ge)
                        v.drain()
                        for tt in range(16):
                            v.tensor_tensor(g16[:, tt, :], g16[:, tt, :], lgS[:, tt, :], ALU.mult)
                with nc.Block() as blk:
                    @blk.vector
                    def _(v):
                        for tt in range(16):
                            v.reduce_sum(gmB[:, tt:tt + 1], g16[:, tt, :], axis=AXX)
                with nc.Block() as blk:
                    @blk.vector
                    def _(v):
                        for tt in range(16):
                            v.reciprocal(gmB[:, tt:tt + 1], gmB[:, tt:tt + 1])
                with nc.Block() as blk:
                    @blk.vector
                    def _(v):
                        for tt in range(16):
                            v.tensor_tensor(g16[:, tt, :], g16[:, tt, :],
                                            bass.AP(gmB, tt, [[16, 128], [0, NE]]), ALU.mult)

                for tt in range(16):
                    with nc.Block() as blk:
                        @blk.tensor
                        def _(t, tt=tt):
                            t.transpose(pE[0:NE, 0:128], g16[:, tt, :], idS[:])
                    with nc.Block() as blk:
                        @blk.vector
                        def _(v, tt=tt):
                            v.tensor_copy(gT8[:, tt * 128:(tt + 1) * 128], pE[0:NE, 0:128])
            if do_header >= 4:
                with nc.Block() as blk:
                    @blk.sync
                    def _(sync):
                        sync.dma_start(out=bass.AP(hscr.tensor, 0, [[T, NE], [1, T]]),
                                       in_=gT8[:, :]).then_inc(sStg, 16)
                        cc.inc("stg", 16)
                with nc.Block() as blk:
                    @blk.sync
                    def _(sync):
                        sync.wait_ge(sStg, cc.get("stg"))

        if do_attn:
            with ExitStack() as ph:
                XR = ph.enter_context(nc.sbuf_tensor("XR", [128, CH, 512], F32))
                Wa = ph.enter_context(nc.sbuf_tensor("Wa", [128, CH, 128], F32))
                Wb = ph.enter_context(nc.sbuf_tensor("Wb", [128, CH, 128], F32))
                WpS = ph.enter_context(nc.sbuf_tensor("WpS", [128, C], F32))
                ktc = ph.enter_context(nc.sbuf_tensor("ktc", [128, 4, 128], F32))
                vtc = ph.enter_context(nc.sbuf_tensor("vtc", [128, 4, 128], F32))
                qaA = ph.enter_context(nc.sbuf_tensor("qaA", [128, T], F32))
                Gbd = ph.enter_context(nc.sbuf_tensor("Gbd", [128, 128], F32))
                ocA = ph.enter_context(nc.sbuf_tensor("ocA", [128, T], F32))

                with nc.Block() as blk:
                    @blk.vector
                    def _(v):
                        v.tensor_single_scalar(bnA[:], bnA[:], 1e-6, ALU.add)
                with nc.Block() as blk:
                    @blk.scalar
                    def _(s):
                        s.activation(qbs2[:], bnA[:], AF.Sqrt)
                with nc.Block() as blk:
                    @blk.vector
                    def _(v):
                        v.reciprocal(qbs2[:], qbs2[:])
                        v.drain()
                        v.tensor_tensor(ob2[:], qbs2[:], qbs2[:], ALU.mult)
                        v.drain()
                        v.tensor_tensor(ob2[:], ob2[:], bnA[:], ALU.mult)
                        v.drain()
                        v.tensor_scalar(ob2[:], ob2[:], -0.5, 1.5, ALU.mult, ALU.add)
                        v.drain()
                        v.tensor_tensor(qbs2[:], qbs2[:], ob2[:], ALU.mult)
                        v.drain()
                        v.tensor_tensor(qbs2[:], qbs2[:], bnG[:], ALU.mult)
                        v.drain()
                        v.tensor_tensor(bnM[:], bnM[:], qbs2[:], ALU.mult)
                        v.drain()
                        v.tensor_tensor(bnB[:], bnB[:], bnM[:], ALU.subtract)
                        v.drain()
                        v.tensor_single_scalar(bnA[:], qbs2[:], 0.25, ALU.mult)
                        v.drain()

                with nc.Block() as blk:
                    ex0 = cc.get("ex")
                    ev0 = cc.get("ev")

                    @blk.vector
                    def _(v, ev0=ev0, ex0=ex0):
                        for c in range(CH):
                            if c >= 1:
                                v.wait_ge(sEx, ex0 + c)
                            v.tensor_tensor(ocA[:, 0:512], XN[:, c, 0:T:4], XN[:, c, 1:T:4], ALU.add)
                            v.drain()
                            v.tensor_tensor(ocA[:, 0:512], ocA[:, 0:512], XN[:, c, 2:T:4], ALU.add)
                            v.drain()
                            v.tensor_tensor(ocA[:, 0:512], ocA[:, 0:512], XN[:, c, 3:T:4], ALU.add)
                            v.drain().then_inc(sEv, 1)

                    @blk.scalar
                    def _(s, ev0=ev0):
                        for c in range(CH):
                            s.wait_ge(sEv, ev0 + c + 1)
                            s.activation(XR[:, c, :], ocA[:, 0:512], AF.Identity,
                                         bias=bnB[:, c:c + 1], scale=bnA[:, c:c + 1]).then_inc(sEx, 1)
                    cc.inc("ev", CH)
                    cc.inc("ex", CH)

                for c in range(CH):
                    with nc.Block() as blk:
                        @blk.sync
                        def _(sync, c=c):
                            sync.dma_start(out=Wa[:], in_=colslice_ap(wkT, c * 128)).then_inc(sLd, 16)
                            cc.inc("ld", 16)
                            sync.dma_start(out=Wb[:], in_=colslice_ap(wvT, c * 128)).then_inc(sLd, 16)
                            cc.inc("ld", 16)
                            sync.dma_start(out=WpS[:], in_=bass.AP(wpT.tensor, c * 128 * C, [[C, 128], [1, C]]))\
                                .then_inc(sLd, 16)
                            cc.inc("ld", 16)

                    with nc.Block() as blk:
                        mm0, ex0 = cc.get("mm"), cc.get("ex")

                        @blk.tensor
                        def _(t, c=c, ex0=ex0):
                            t.wait_ge(sLd, cc.get("ld"))
                            for tt in range(4):
                                if tt >= 2:
                                    t.wait_ge(sEx, ex0 + tt - 1)
                                bank = (pA0, pA1)[tt % 2]
                                for ic in range(CH):
                                    t.matmul(bank[:, 0:128], XR[:, ic, tt * 128:(tt + 1) * 128],
                                             Wa[:, ic, :], start=(ic == 0), stop=False,
                                             skip_group_check=True)
                                for ic in range(CH):
                                    ins = t.matmul(bank[:, 128:256], XR[:, ic, tt * 128:(tt + 1) * 128],
                                                   Wb[:, ic, :], start=(ic == 0), stop=(ic == CH - 1),
                                                   skip_group_check=True)
                                ins.then_inc(sMm, 1)

                        @blk.scalar
                        def _(s, c=c, mm0=mm0):
                            for tt in range(4):
                                s.wait_ge(sMm, mm0 + tt + 1)
                                bank = (pA0, pA1)[tt % 2]
                                s.copy(ktc[:, tt, :], bank[:, 0:128])
                                s.copy(vtc[:, tt, :], bank[:, 128:256])
                                s.drain().then_inc(sEx, 1)
                        cc.inc("mm", 4)
                        cc.inc("ex", 4)

                    with nc.Block() as blk:
                        @blk.tensor
                        def _(t, c=c):
                            for h in range(4):
                                for mc in range(4):
                                    t.matmul(pE[32 * h:32 * h + 32, 0:32],
                                             ktc[:, mc, 32 * h:32 * h + 32],
                                             vtc[:, mc, 32 * h:32 * h + 32],
                                             start=(mc == 0), stop=(mc == 3),
                                             tile_position=(0, 32 * h), skip_group_check=True)

                        @blk.vector
                        def _(v, c=c):
                            v.memset(Gbd[:], 0.0)

                        @blk.sync
                        def _(sync, c=c):
                            sync.dma_start(out=Wa[:], in_=colslice_ap(wqT, c * 128)).then_inc(sLd, 16)
                            cc.inc("ld", 16)
                    with nc.Block() as blk:
                        @blk.vector
                        def _(v, c=c):
                            for h in range(4):
                                v.tensor_copy(Gbd[32 * h:32 * h + 32, 32 * h:32 * h + 32],
                                              pE[32 * h:32 * h + 32, 0:32])

                    with nc.Block() as blk:
                        mm0, ex0 = cc.get("mm"), cc.get("ex")

                        @blk.tensor
                        def _(t, c=c, ex0=ex0):
                            t.wait_ge(sLd, cc.get("ld"))
                            for tb in range(TB):
                                if tb >= 2:
                                    t.wait_ge(sEx, ex0 + tb - 1)
                                bank = (pA0, pA1)[tb % 2]
                                for ic in range(CH):
                                    ins = t.matmul(bank[:, :], Wa[:, ic, :], XN[:, ic, tb * 512:(tb + 1) * 512],
                                                   start=(ic == 0), stop=(ic == CH - 1))
                                ins.then_inc(sMm, 1)

                        @blk.scalar
                        def _(s, c=c, mm0=mm0):
                            for tb in range(TB):
                                s.wait_ge(sMm, mm0 + tb + 1)
                                bank = (pA0, pA1)[tb % 2]
                                s.activation(qaA[:, tb * 512:(tb + 1) * 512], bank[:, :], AF.Identity,
                                             bias=zcol[:, :], scale=ATT_SCALE).then_inc(sEx, 1)
                        cc.inc("mm", TB)
                        cc.inc("ex", TB)

                    with nc.Block() as blk:
                        mm0, ex0 = cc.get("mm"), cc.get("ex")

                        @blk.tensor
                        def _(t, c=c, ex0=ex0):
                            for tb in range(TB):
                                if tb >= 2:
                                    t.wait_ge(sEx, ex0 + tb - 1)
                                bank = (pB0, pB1)[tb % 2]
                                t.matmul(bank[:, :], Gbd[:], qaA[:, tb * 512:(tb + 1) * 512],
                                         start=True, stop=True).then_inc(sMm, 1)

                        @blk.scalar
                        def _(s, c=c, mm0=mm0):
                            for tb in range(TB):
                                s.wait_ge(sMm, mm0 + tb + 1)
                                bank = (pB0, pB1)[tb % 2]
                                s.copy(ocA[:, tb * 512:(tb + 1) * 512], bank[:, :])
                                s.drain().then_inc(sEx, 1)
                        cc.inc("mm", TB)
                        cc.inc("ex", TB)

                    with nc.Block() as blk:
                        mm0, ev0 = cc.get("mm"), cc.get("ev")

                        @blk.tensor
                        def _(t, c=c, ev0=ev0):
                            k = 0
                            for tile in range(CH):
                                for tb in range(TB):
                                    if k >= 2:
                                        t.wait_ge(sEv, ev0 + k - 1)
                                    bank = (pA0, pA1)[k % 2]
                                    t.matmul(bank[:, :], WpS[:, tile * 128:(tile + 1) * 128],
                                             ocA[:, tb * 512:(tb + 1) * 512], start=True, stop=True).then_inc(sMm, 1)
                                    k += 1

                        @blk.vector
                        def _(v, c=c, mm0=mm0):
                            k = 0
                            for tile in range(CH):
                                for tb in range(TB):
                                    v.wait_ge(sMm, mm0 + k + 1)
                                    bank = (pA0, pA1)[k % 2]
                                    v.tensor_tensor(X[:, tile, tb * 512:(tb + 1) * 512],
                                                    X[:, tile, tb * 512:(tb + 1) * 512],
                                                    bank[:, :], ALU.add).then_inc(sEv, 1)
                                    k += 1
                        cc.inc("mm", CH * TB)
                        cc.inc("ev", CH * TB)

        if n_experts:
            with ExitStack() as ph:
                Wea = ph.enter_context(nc.sbuf_tensor("Wea", [128, CH, 128], F32))
                Web = ph.enter_context(nc.sbuf_tensor("Web", [128, CH, 128], F32))
                WO = ph.enter_context(nc.sbuf_tensor("WO", [128, C], F32))
                qae = ph.enter_context(nc.sbuf_tensor("qae", [128, T], F32))
                kae = ph.enter_context(nc.sbuf_tensor("kae", [128, T], F32))
                va = ph.enter_context(nc.sbuf_tensor("va", [128, 16, 130], F32))
                P = ph.enter_context(nc.sbuf_tensor("P", [128, 2, 2, 512], F32))
                stg = ph.enter_context(nc.sbuf_tensor("stg", [128, 2, 512], F32))
                otmp = ph.enter_context(nc.sbuf_tensor("otmp", [64, 512], F32))
                oche = ph.enter_context(nc.sbuf_tensor("oche", [128, T], F32))

                with nc.Block() as blk:
                    @blk.vector
                    def _(v):
                        v.memset(va[:, :, 64:65], 1.0)
                        v.memset(va[:, :, 129:130], 1.0)

                for e in range(n_experts):
                    with nc.Block() as blk:
                        @blk.gpsimd
                        def _(g, e=e):
                            g.dma_start(out=grow[:, :],
                                        in_=bass.AP(hscr.tensor, e * T, [[T, 1], [1, T]]))\
                                .then_inc(sBc, 16)
                            cc.inc("bc", 16)

                        @blk.sync
                        def _(sync, e=e):
                            for dst, src in ((qbS, qbD), (kbS, kbD), (vbS, vbD), (obS, obD)):
                                sync.dma_start(out=dst[:], in_=src[e, :, :]).then_inc(sLd, 16)
                                cc.inc("ld", 16)

                    with nc.Block() as blk:
                        @blk.vector
                        def _(v, e=e):
                            v.wait_ge(sLd, cc.get("ld"))
                            v.wait_ge(sBc, cc.get("bc"))
                            v.tensor_single_scalar(qbs2[:], qbS[:], FFN_SCALE, ALU.mult)

                    with nc.Block() as blk:
                        mm0, ex0, pv0, ev0, bc0 = (cc.get("mm"), cc.get("ex"), cc.get("pv"),
                                                   cc.get("ev"), cc.get("bc"))
                        wa0, wb0, wo0 = cc.get("wa"), cc.get("wb"), cc.get("wo")

                        @blk.sync
                        def _(sync, e=e, mm0=mm0):
                            for hp in range(NHP):
                                off = e * C * C + hp * 128
                                mmB = mm0 + 120 * hp
                                if hp > 0:
                                    sync.wait_ge(sMm, mm0 + 120 * (hp - 1) + 24)
                                sync.dma_start(out=Wea[:], in_=bass.AP(qWT.tensor, off, [[C, 128], [C * 128, CH], [1, 128]]))\
                                    .then_inc(sWa, 16)
                                sync.dma_start(out=Web[:], in_=bass.AP(kWT.tensor, off, [[C, 128], [C * 128, CH], [1, 128]]))\
                                    .then_inc(sWb, 16)
                                sync.dma_start(out=vbRow[:, :], in_=vbR[e:e + 1, hp * 128:(hp + 1) * 128])\
                                    .then_inc(sWo, 16)
                                if hp > 0:
                                    sync.wait_ge(sMm, mm0 + 120 * hp)
                                sync.dma_start(out=WO[:], in_=bass.AP(oWT.tensor, e * C * C + hp * 128 * C, [[C, 128], [1, C]]))\
                                    .then_inc(sWo, 16)
                                sync.wait_ge(sMm, mmB + 4)
                                sync.dma_start(out=Wea[:], in_=bass.AP(vWT.tensor, off, [[C, 128], [C * 128, CH], [1, 128]]))\
                                    .then_inc(sWa, 16)

                        @blk.tensor
                        def _(t, e=e, mm0=mm0, ex0=ex0, pv0=pv0, ev0=ev0, bc0=bc0,
                              wa0=wa0, wb0=wb0, wo0=wo0):
                            for hp in range(NHP):
                                mmB = mm0 + 120 * hp
                                exB = ex0 + 152 * hp
                                pvB = pv0 + 64 * hp
                                evB = ev0 + 44 * hp
                                bcB = bc0 + 192 * hp
                                X0 = exB + 24
                                t.wait_ge(sEv, ev0 + 44 * hp)
                                t.wait_ge(sWa, wa0 + 32 * hp + 16)
                                for g8 in range(2 * TB):
                                    if g8 == TB:
                                        t.wait_ge(sWb, wb0 + 16 * hp + 16)
                                    if g8 >= 2:
                                        t.wait_ge(sEx, exB + g8 - 1)
                                    W = Wea if g8 < TB else Web
                                    tb = g8 % TB
                                    bank = (pA0, pA1)[g8 % 2]
                                    for ic in range(CH):
                                        ins = t.matmul(bank[:, :], W[:, ic, :],
                                                       XN[:, ic, tb * 512:(tb + 1) * 512],
                                                       start=(ic == 0), stop=(ic == CH - 1))
                                    ins.then_inc(sMm, 1)
                                t.wait_ge(sWa, wa0 + 32 * hp + 32)
                                t.wait_ge(sWo, wo0 + 32 * hp + 32)
                                for tt in range(16):
                                    t.wait_ge(sEx, exB + 2 * TB + tt - 1)
                                    bank = (pA0, pA1)[tt % 2]
                                    for ic in range(CH):
                                        t.matmul(bank[:, 0:128],
                                                 XN[:, ic, tt * 128:(tt + 1) * 128],
                                                 Wea[:, ic, :],
                                                 start=(ic == 0), stop=False,
                                                 skip_group_check=True)
                                    t.matmul(bank[:, 0:128], onesr[0:1, :],
                                             vbRow[0:1, 0:128],
                                             start=False, stop=True,
                                             skip_group_check=True).then_inc(sMm, 1)
                                for qb_ in range(TB):
                                    pv0b, pv1b = ((pC0, pC1), (pD, pE))[qb_ % 2]
                                    if qb_ >= 2:
                                        t.wait_ge(sEv, evB + 3 * qb_ - 4)
                                    for kc in range(KC):
                                        t.wait_ge(sEx, X0 + 32 * qb_ + 2 * kc)
                                        t.matmul(pB0[:, :], kae[0:64, kc * 128:(kc + 1) * 128],
                                                 qae[0:64, qb_ * 512:(qb_ + 1) * 512],
                                                 start=True, stop=True, skip_group_check=True)
                                        t.matmul(pB1[:, :], kae[64:128, kc * 128:(kc + 1) * 128],
                                                 qae[64:128, qb_ * 512:(qb_ + 1) * 512],
                                                 start=True, stop=True, skip_group_check=True).then_inc(sMm, 1)
                                        if kc > 0:
                                            t.matmul(pv0b[0:65, :], va[:, kc - 1, 0:65],
                                                     P[:, 0, (kc - 1) % 2, :],
                                                     start=(kc - 1 == 0), stop=False,
                                                     skip_group_check=True)
                                            t.matmul(pv1b[0:65, :], va[:, kc - 1, 65:130],
                                                     P[:, 1, (kc - 1) % 2, :],
                                                     start=(kc - 1 == 0), stop=False,
                                                     skip_group_check=True).then_inc(sPv, 1)
                                    t.wait_ge(sEx, X0 + 32 * qb_ + 32)
                                    t.matmul(pv0b[0:65, :], va[:, KC - 1, 0:65], P[:, 0, (KC - 1) % 2, :],
                                             start=False, stop=True, skip_group_check=True)
                                    t.matmul(pv1b[0:65, :], va[:, KC - 1, 65:130], P[:, 1, (KC - 1) % 2, :],
                                             start=False, stop=True, skip_group_check=True).then_inc(sPv, 1)
                                t.wait_ge(sEv, evB + 12)
                                t.wait_ge(sBc, bcB + 192)
                                k = 0
                                for tile in range(CH):
                                    for tb in range(TB):
                                        if k >= 2:
                                            t.wait_ge(sEv, evB + 12 + k - 1)
                                        bank = (pA0, pA1)[k % 2]
                                        t.matmul(bank[:, :], WO[:, tile * 128:(tile + 1) * 128],
                                                 oche[:, tb * 512:(tb + 1) * 512],
                                                 start=True, stop=True).then_inc(sMm, 1)
                                        k += 1

                        @blk.scalar
                        def _(s, e=e, mm0=mm0, ex0=ex0, pv0=pv0, bc0=bc0):
                            for hp in range(NHP):
                                mmB = mm0 + 120 * hp
                                exB = ex0 + 152 * hp
                                pvB = pv0 + 64 * hp
                                bcB = bc0 + 192 * hp
                                for g8 in range(2 * TB):
                                    s.wait_ge(sMm, mmB + g8 + 1)
                                    tb = g8 % TB
                                    bank = (pA0, pA1)[g8 % 2]
                                    dst, bias, scale = ((qae, qbs2, FFN_SCALE) if g8 < TB
                                                        else (kae, kbS, 1.0))
                                    s.activation(dst[:, tb * 512:(tb + 1) * 512], bank[:, :], AF.Identity,
                                                 bias=bias[:, hp:hp + 1], scale=scale).then_inc(sEx, 1)
                                for tt in range(16):
                                    s.wait_ge(sMm, mmB + 2 * TB + tt + 1)
                                    bank = (pA0, pA1)[tt % 2]
                                    s.copy(va[:, tt, 0:64], bank[:, 0:64])
                                    s.copy(va[:, tt, 65:129], bank[:, 64:128]).then_inc(sEx, 1)
                                for qb_ in range(TB):
                                    for kc in range(KC):
                                        s.wait_ge(sMm, mmB + 24 + 16 * qb_ + kc + 1)
                                        if kc >= 2:
                                            s.wait_ge(sPv, pvB + 16 * qb_ + kc - 1)
                                        elif kc == 0:
                                            s.wait_ge(sPv, max(0, pvB + 16 * qb_ - 1))
                                        else:
                                            s.wait_ge(sPv, pvB + 16 * qb_)
                                        s.activation(P[:, 0, kc % 2, :], pB0[:, :], AF.Exp,
                                                     bias=zcol[:, :], scale=1.0).then_inc(sEx, 1)
                                        s.activation(P[:, 1, kc % 2, :], pB1[:, :], AF.Exp,
                                                     bias=zcol[:, :], scale=1.0).then_inc(sEx, 1)

                        @blk.vector
                        def _(v, e=e, mm0=mm0, pv0=pv0, ev0=ev0, bc0=bc0):
                            for hp in range(NHP):
                                mmB = mm0 + 120 * hp
                                pvB = pv0 + 64 * hp
                                evB = ev0 + 44 * hp
                                bcB = bc0 + 192 * hp
                                for qb_ in range(TB):
                                    sl = slice(qb_ * 512, (qb_ + 1) * 512)
                                    par = qb_ % 2
                                    pv0b, pv1b = ((pC0, pC1), (pD, pE))[par]
                                    v.wait_ge(sPv, pvB + 16 * (qb_ + 1))
                                    v.reciprocal(rrow[0:1, 0, :], pv0b[64:65, :])
                                    v.reciprocal(rrow[0:1, 1, :], pv1b[64:65, :]).then_inc(sEv, 1)
                                    v.wait_ge(sBc, bcB + 48 * qb_ + 32)
                                    v.tensor_tensor(oche[0:64, sl], pv0b[0:64, :], stg[0:64, par, :], ALU.mult)
                                    v.tensor_copy(otmp[:, :], pv1b[0:64, :]).then_inc(sEv, 1)
                                    v.wait_ge(sBc, bcB + 48 * qb_ + 48)
                                    v.tensor_tensor(oche[64:128, sl], oche[64:128, sl],
                                                    stg[64:128, par, :], ALU.mult).then_inc(sEv, 1)
                                k = 0
                                for tile in range(CH):
                                    for tb in range(TB):
                                        v.wait_ge(sMm, mmB + 88 + k + 1)
                                        bank = (pA0, pA1)[k % 2]
                                        v.tensor_tensor(X[:, tile, tb * 512:(tb + 1) * 512],
                                                        X[:, tile, tb * 512:(tb + 1) * 512],
                                                        bank[:, :], ALU.add).then_inc(sEv, 1)
                                        k += 1

                        @blk.gpsimd
                        def _(g, e=e, ev0=ev0, bc0=bc0):
                            for hp in range(NHP):
                                evB = ev0 + 44 * hp
                                for qb_ in range(TB):
                                    sl = slice(qb_ * 512, (qb_ + 1) * 512)
                                    par = qb_ % 2
                                    g.wait_ge(sEv, evB + 3 * qb_ + 1)
                                    g.tensor_tensor(rrow[0:1, 0, :], rrow[0:1, 0, :], grow[0:1, sl], ALU.mult)
                                    g.tensor_tensor(rrow[0:1, 1, :], rrow[0:1, 1, :], grow[0:1, sl], ALU.mult)
                                    g.drain()
                                    g.dma_start(out=stg[0:64, par, :],
                                                in_=bass.AP(rrow, 0, [[1024, 1], [0, 64], [1, 512]]))\
                                        .then_inc(sBc, 16)
                                    g.dma_start(out=stg[64:128, par, :],
                                                in_=bass.AP(rrow, 512, [[1024, 1], [0, 64], [1, 512]]))\
                                        .then_inc(sBc, 16)
                                    g.wait_ge(sEv, evB + 3 * qb_ + 2)
                                    g.dma_start(out=oche[64:128, sl], in_=otmp[:, :])\
                                        .then_inc(sBc, 16)
                        cc.inc("mm", 120 * NHP)
                        cc.inc("ex", 152 * NHP)
                        cc.inc("pv", 64 * NHP)
                        cc.inc("ev", 44 * NHP)
                        cc.inc("bc", 192 * NHP)
                        cc.inc("wa", 32 * NHP)
                        cc.inc("wb", 16 * NHP)
                        cc.inc("wo", 32 * NHP)

                    with nc.Block() as blk:
                        @blk.vector
                        def _(v, e=e):
                            v.tensor_copy(ob2[:], obS[:])
                    with nc.Block() as blk:
                        bc0 = cc.get("bc")

                        @blk.gpsimd
                        def _(g, e=e):
                            for tb in range(TB):
                                g.dma_start(out=stg[:, 0, :],
                                            in_=bass.AP(grow, tb * 512, [[T, 1], [0, 128], [1, 512]]))\
                                    .then_inc(sBc, 16)
                                g.wait_ge(sEv, cc.get("ev") + tb + 1)

                        @blk.vector
                        def _(v, e=e, bc0=bc0):
                            for tb in range(TB):
                                v.wait_ge(sBc, bc0 + 16 * (tb + 1))
                                for tile in range(CH):
                                    ins = v.scalar_tensor_tensor(X[:, tile, tb * 512:(tb + 1) * 512],
                                                                 stg[:, 0, :], ob2[:, tile:tile + 1],
                                                                 X[:, tile, tb * 512:(tb + 1) * 512],
                                                                 ALU.mult, ALU.add)
                                ins.then_inc(sEv, 1)
                        cc.inc("bc", 16 * TB)
                        cc.inc("ev", TB)

        with nc.Block() as blk:
            @blk.sync
            def _(sync):
                nst = 16
                sync.dma_start(out=bass.AP(yT.tensor, 0, [[T, 128], [T * 128, CH], [1, T]]),
                               in_=X[:]).then_inc(sSt, 16)
                if dbg_gates:
                    sync.dma_start(out=gdbg[:, :], in_=gT8[:, :]).then_inc(sSt, 16)
                    sync.dma_start(out=ldbg[:, :, :], in_=lgS[:, :, :]).then_inc(sSt, 16)
                    sync.dma_start(out=g16dbg[:, :, :], in_=g16[:, :, :]).then_inc(sSt, 16)
                    nst += 48
                sync.wait_ge(sSt, nst)

    return nc


def kernel(**inputs):
    I = {k: np.ascontiguousarray(np.asarray(v), dtype=np.float32) for k, v in inputs.items()}
    x = I["L0_attn_x"]
    cT = lambda a: np.ascontiguousarray(a.T)
    chunk1 = lambda a: np.ascontiguousarray(a.reshape(-1, 128).T)
    chunk2 = lambda a: np.ascontiguousarray(a.reshape(NE, -1, 128).transpose(0, 2, 1))
    shared = dict(
        wqT=cT(I["L0_attn_Wq"]), wkT=cT(I["L0_attn_Wk"]),
        wvT=cT(I["L0_attn_Wv"]), wpT=cT(I["L0_attn_Wp"]),
        bp=chunk1(I["L0_attn_bp"]), bn_mean=chunk1(I["L0_attn_bn_mean"]),
        bn_var=chunk1(I["L0_attn_bn_var"]), bn_g=chunk1(I["L0_attn_bn_gamma"]),
        bn_b=chunk1(I["L0_attn_bn_beta"]),
        rw1T=cT(I["L0_ffn_rW1"]), rb1=chunk1(I["L0_ffn_rb1"]),
        lng=chunk1(I["L0_ffn_ln_g"]), lnb=chunk1(I["L0_ffn_ln_b"]),
        rw2T=np.ascontiguousarray(cT(I["L0_ffn_rW2"]).reshape(HC4, 128, NE).transpose(1, 0, 2)),
        rb2=I["L0_ffn_rb2"].reshape(1, NE),
        qWT=np.ascontiguousarray(I["L0_ffn_qW"].transpose(0, 2, 1)),
        kWT=np.ascontiguousarray(I["L0_ffn_kW"].transpose(0, 2, 1)),
        vWT=np.ascontiguousarray(I["L0_ffn_vW"].transpose(0, 2, 1)),
        oWT=np.ascontiguousarray(I["L0_ffn_oW"].transpose(0, 2, 1)),
        qb=chunk2(I["L0_ffn_qb"]), kb=chunk2(I["L0_ffn_kb"]),
        vb=chunk2(I["L0_ffn_vb"]), ob=chunk2(I["L0_ffn_ob"]),
        vbR=I["L0_ffn_vb"],
        ident=np.eye(128, dtype=np.float32),
    )
    nc = build_nc()
    in_maps = [dict(shared, xT=np.ascontiguousarray(x[b].T)) for b in range(8)]
    res = run_bass_kernel_spmd(nc, in_maps, core_ids=list(range(8)))
    y = np.stack([np.asarray(r["yT"]).T for r in res.results])
    return np.ascontiguousarray(y).astype(np.float32)
